# version 03
# baseline (speedup 1.0000x reference)
import numpy as np
import ml_dtypes
from contextlib import ExitStack

import concourse.bass as bass
import concourse.mybir as mybir
from concourse.bass_utils import run_bass_kernel_spmd

F32 = mybir.dt.float32
BF16 = mybir.dt.bfloat16
I32 = mybir.dt.int32
AF = mybir.ActivationFunctionType
ALU = mybir.AluOpType
AX = mybir.AxisListType
BF = ml_dtypes.bfloat16

D = 2048
S = 4096
B = 4
T = 2048
NCORE = 8
RG_PAIRS = [[0, 1], [2, 3], [4, 5], [6, 7]]
SOFTMAX_SHIFT = 16.0

ENGS = ("tensor", "vector", "scalar", "gpsimd", "sync")


class Prog:

    def __init__(self, name="prog"):
        self.nc = bass.Bass("TRN2", target_bir_lowering=False, debug=False)
        self.es = ExitStack()
        self._sem_id = 0
        self._name_id = 0
        self.consts = {}
        self._const_vals = set([0.0, 1.0])
        self.steps = []
        self.dma_counts = {}
        self.built = False

    def name(self, prefix):
        self._name_id += 1
        return f"{prefix}_{self._name_id}"

    def sem(self, name=None):
        self._sem_id += 1
        return self.es.enter_context(self.nc.semaphore(name or f"s{self._sem_id}"))

    def sbuf(self, prefix, shape, dtype, scope=None):
        cm = self.nc.sbuf_tensor(self.name(prefix), list(shape), dtype)
        if scope is not None:
            t = cm.__enter__()
            scope.append(cm)
            return t
        return self.es.enter_context(cm)

    def release(self, scope):
        for cm in reversed(scope):
            cm.__exit__(None, None, None)
        scope.clear()

    def psum(self, prefix, shape, dtype=F32, scope=None):
        cm = self.nc.psum_tensor(self.name(prefix), list(shape), dtype)
        if scope is not None:
            t = cm.__enter__()
            scope.append(cm)
            return t
        return self.es.enter_context(cm)

    def dram(self, name, shape, dtype, kind="Internal"):
        return self.nc.dram_tensor(name, list(shape), dtype, kind=kind).ap()

    def want_const(self, *vals):
        for v in vals:
            self._const_vals.add(float(v))

    def c(self, val):
        return self.consts[float(val)]

    def add(self, engine, fn, deps=(), small=False, dma=False, key=None,
            cc=False):
        t = len(self.steps)
        self.steps.append(dict(e=engine, fn=fn,
                               deps=[d for d in deps if d is not None],
                               small=small, dma=dma, inc=False, val=0, key=key,
                               cc=cc, snap=dict(self.dma_counts)))
        if dma:
            self.dma_counts[key] = self.dma_counts.get(key, 0) + 1
        return t

    def ccadd(self, fn, deps=()):
        self._cc_n = getattr(self, "_cc_n", 0) + 1
        return self.add("gpsimd", fn, deps, dma=True, key=("cc", self._cc_n),
                        cc=True)

    def dma(self, fn, deps=(), q="sync", key=None):
        if key is None:
            key = q
        return self.add(q, fn, deps, dma=True, key=key)

    def last(self, engine=None):
        if engine is None:
            return len(self.steps) - 1 if self.steps else None
        for t in range(len(self.steps) - 1, -1, -1):
            if self.steps[t]["e"] == engine:
                return t
        return None

    def fence(self):
        return [self.last(e) for e in ENGS if self.last(e) is not None]

    def build(self, core_dep_check=True, debug_serialize=False):
        assert not self.built
        self.built = True
        if debug_serialize:
            for t in range(1, len(self.steps)):
                self.steps[t]["deps"] = sorted(set(self.steps[t]["deps"] + [t - 1]))
            for s_ in self.steps:
                s_["small"] = True
        nc = self.nc
        for s in self.steps:
            if s["dma"]:
                s["inc"] = True
            for d in s["deps"]:
                ds = self.steps[d]
                if ds["e"] != s["e"] or ds["small"] or ds["dma"]:
                    ds["inc"] = True
        counts = {e: 0 for e in ENGS}
        dcounts = {}
        for s in self.steps:
            if s["inc"]:
                if s.get("cc"):
                    dcounts[s["key"]] = 1
                    s["val"] = 1
                elif s["dma"]:
                    dcounts[s["key"]] = dcounts.get(s["key"], 0) + 16
                    s["val"] = dcounts[s["key"]]
                else:
                    counts[s["e"]] += 1
                    s["val"] = counts[s["e"]]
        sems = {e: self.sem("clk_" + e) for e in ENGS}
        dkeys = sorted(set(s_["key"] for s_ in self.steps if s_["dma"]), key=str)
        dsems = {k: self.sem(f"dk{i}") for i, k in enumerate(dkeys)}

        self.nc.default_side = "right"
        cvals = sorted(self._const_vals)
        for i, v in enumerate(cvals):
            self.consts[v] = self.sbuf("const", [128, 1], F32)[:, 0:1]
        self.ones_bf = self.sbuf("ones_bf", [1, 128], BF16)
        self.ident_bf = self.sbuf("ident_bf", [128, 128], BF16)
        self.sel8_bf = self.sbuf("sel8_bf", [8, 8], BF16)
        self.nc.default_side = "left"
        iden_in = self.dram("c_ident", [128, 128], F32, kind="ExternalInput")
        sel8_in = self.dram("c_sel8", [8, 8], F32, kind="ExternalInput")

        per_engine = {e: [t for t, s in enumerate(self.steps) if s["e"] == e]
                      for e in ENGS}
        hiw = {}
        csem = self.sem("cinit")

        with nc.Block() as block:
            @block.gpsimd
            def _(g):
                g.dma_start(out=self.ident_bf[:, :], in_=iden_in[:, :]).then_inc(csem, 16)
                g.dma_start(out=self.sel8_bf[:, :], in_=sel8_in[:, :]).then_inc(csem, 16)

            @block.vector
            def _(v):
                for val, ap in self.consts.items():
                    if val != 0.0:
                        v.memset(ap, val)
                    else:
                        v.memset(ap, 0.0)
                v.memset(self.ones_bf[0:1, :], 1.0).then_inc(csem, 1)

            def run_engine(eng, toks):
                eng.wait_ge(csem, 33)
                for t in toks:
                    s = self.steps[t]
                    for d in s["deps"]:
                        ds = self.steps[d]
                        if ds["e"] == s["e"] and not ds["inc"]:
                            continue
                        assert ds["inc"], f"step {d} lacks inc but step {t} depends on it"
                        if ds.get("cc"):
                            val = 1
                            key = (s["e"], "d", ds["key"])
                            tgt = dsems[ds["key"]]
                        elif ds["dma"]:
                            val = 16 * s["snap"][ds["key"]]
                            assert val >= ds["val"]
                            key = (s["e"], "d", ds["key"])
                            tgt = dsems[ds["key"]]
                        else:
                            val = ds["val"]
                            key = (s["e"], ds["e"])
                            tgt = sems[ds["e"]]
                        if hiw.get(key, -1) < val:
                            eng.wait_ge(tgt, val)
                            hiw[key] = val
                    inst = s["fn"](eng)
                    if s["inc"]:
                        assert inst is not None, f"step {t} must return its instruction"
                        if s.get("cc"):
                            inst.then_inc(dsems[s["key"]])
                        elif s["dma"]:
                            inst.then_inc(dsems[s["key"]], 16)
                        else:
                            inst.then_inc(sems[s["e"]], 1)

            @block.tensor
            def _(t):
                run_engine(t, per_engine["tensor"])

            @block.vector
            def _(v):
                run_engine(v, per_engine["vector"])

            @block.scalar
            def _(a):
                run_engine(a, per_engine["scalar"])

            @block.gpsimd
            def _(g):
                run_engine(g, per_engine["gpsimd"])

            @block.sync
            def _(sy):
                run_engine(sy, per_engine["sync"])
                for e in ENGS:
                    if counts[e] > 0 and hiw.get(("sync", e), -1) < counts[e]:
                        sy.wait_ge(sems[e], counts[e])
                for kk, vv in dcounts.items():
                    if vv > 0 and hiw.get(("sync", "d", kk), -1) < vv:
                        sy.wait_ge(dsems[kk], vv)

        return self.nc


def ceil_div(a, b):
    return (a + b - 1) // b



def emit_ln(p, x_dram, xn_dram, n_tok, deps=(), eps=1e-5, fold=False,
            g_dram=None, b_dram=None):
    nt = n_tok // 128
    p.want_const(eps, eps * eps)
    lscope = []
    xs = [p.sbuf("ln_x", [128, D], F32, lscope) for _ in range(3)]
    sq = p.sbuf("ln_sq", [128, D], F32, lscope)
    xn = [p.sbuf("ln_xn", [128, D], BF16, lscope) for _ in range(3)]
    stats = [p.sbuf("ln_st", [128, 8], F32, lscope) for _ in range(3)]
    if fold:
        gb = p.sbuf("ln_g", [128, D], F32, lscope)
        bb = p.sbuf("ln_b", [128, D], F32, lscope)
        grow = p.sbuf("ln_gr", [1, D], BF16, lscope)
        pgb = [p.psum("ln_pgb", [128, 512], scope=lscope) for _ in range(2)]
        t2 = [p.sbuf("ln_t2", [128, D], F32, lscope) for _ in range(2)]
        gbdep = []
        mm_toks = {0: [None, None], 1: [None, None]}
        prev_load_gate = list(deps)
        tload_prev = None
        for which, dst in ((0, gb), (1, bb)):
            src = g_dram if which == 0 else b_dram
            tload = p.dma(lambda e, src=src: e.dma_start(
                out=grow[0:1, 0:D], in_=src[None, :]), prev_load_gate, q="gpsimd",
                key="ln_gb")
            cp_toks = []
            for j in range(4):
                bank = j % 2
                dep_cp = [cp_toks[j - 2]] if j >= 2 else []
                tm = p.add("tensor", lambda e, j=j, bank=bank: e.matmul(
                    pgb[bank][:, :], p.ones_bf[0:1, :],
                    grow[0:1, j * 512:(j + 1) * 512],
                    start=True, stop=True), [tload] + dep_cp)
                tc = p.add("vector", lambda e, j=j, bank=bank, dst=dst: e.tensor_copy(
                    dst[:, j * 512:(j + 1) * 512], pgb[bank][:]), [tm])
                cp_toks.append(tc)
            prev_load_gate = [p.last("tensor")]
            gbdep = gbdep + cp_toks[-2:]
    else:
        gbdep = []

    last_store = [None, None, None]
    last_apply = [None, None, None]
    out_toks = []
    for i in range(nt):
        k = i % 3
        ldd = list(deps) + [t for t in (last_store[k], last_apply[k]) if t is not None]
        tl = p.dma(lambda e, i=i, k=k: e.dma_start(
            out=xs[k][:], in_=x_dram[i * 128:(i + 1) * 128, :]), ldd,
            key=("ln_x", k))
        tsum = p.add("vector", lambda e, k=k: e.reduce_sum(
            stats[k][:, 0:1], xs[k][:], axis=AX.X), [tl], small=True)
        tmu = p.add("vector", lambda e, k=k: e.tensor_scalar_mul(
            stats[k][:, 2:3], stats[k][:, 0:1], 1.0 / D), [tsum], small=True)
        tsq = p.add("scalar", lambda e, k=k: e.activation(
            sq[:], xs[k][:], AF.Square, accum_out=stats[k][:, 1:2]), [tl], small=True)
        tmu2 = p.add("vector", lambda e, k=k: e.tensor_mul(
            stats[k][:, 6:7], stats[k][:, 2:3], stats[k][:, 2:3]), [tmu], small=True)
        tv1 = p.add("vector", lambda e, k=k: e.tensor_scalar_mul(
            stats[k][:, 3:4], stats[k][:, 1:2], 1.0 / D), [tsq, tmu2], small=True)
        tvar = p.add("vector", lambda e, k=k: e.tensor_sub(
            stats[k][:, 3:4], stats[k][:, 3:4], stats[k][:, 6:7]), [tv1], small=True)
        sc = (1.0 + eps) if fold else 1.0
        bi = (eps * eps) if fold else eps
        tln = p.add("scalar", lambda e, k=k, sc=sc, bi=bi: e.activation(
            stats[k][:, 4:5], stats[k][:, 3:4], AF.Ln, bias=p.c(bi), scale=sc),
            [tvar], small=True)
        trs = p.add("scalar", lambda e, k=k: e.activation(
            stats[k][:, 5:6], stats[k][:, 4:5], AF.Exp, scale=-0.5), [tln], small=True)
        if fold:
            ta1 = p.add("vector", lambda e, k=k: e.tensor_scalar(
                t2[k % 2][:], xs[k][:], stats[k][:, 2:3], stats[k][:, 5:6],
                ALU.subtract, ALU.mult), [trs, tl] + gbdep +
                ([last_apply[(k + 1) % 3]] if last_apply[(k + 1) % 3] else []))
            ta2 = p.add("vector", lambda e, k=k: e.tensor_mul(
                t2[k % 2][:], t2[k % 2][:], gb[:]), [ta1])
            tap = p.add("vector", lambda e, k=k: e.tensor_add(
                xn[k][:], t2[k % 2][:], bb[:]), [ta2])
        else:
            tap = p.add("vector", lambda e, k=k: e.tensor_scalar(
                xn[k][:], xs[k][:], stats[k][:, 2:3], stats[k][:, 5:6],
                ALU.subtract, ALU.mult), [trs, tl])
        last_apply[k] = tap
        ts = p.dma(lambda e, i=i, k=k: e.dma_start(
            out=xn_dram[i * 128:(i + 1) * 128, :], in_=xn[k][:]), [tap],
            key="ln_st")
        last_store[k] = ts
        out_toks.append(ts)
    p.fence()
    p.release(lscope)
    return out_toks[-1]



def build_ln_launch(fold=False):
    p = Prog()
    x = p.dram("x", [T, D], F32, kind="ExternalInput")
    xn = p.dram("xn", [T, D], BF16, kind="ExternalOutput")
    if fold:
        g = p.dram("g", [D], F32, kind="ExternalInput")
        b = p.dram("b", [D], F32, kind="ExternalInput")
        emit_ln(p, x, xn, T, fold=True, g_dram=g, b_dram=b)
    else:
        emit_ln(p, x, xn, T)
    return p.build()


IDENT_NP = np.eye(128, dtype=np.float32)


SEL8_NP = np.eye(8, dtype=np.float32)
INV64_NP = (1.0 / 10000.0 ** (np.arange(64, dtype=np.float64) / 64.0)).astype(
    np.float32).reshape(1, 64)


def _run(nc, in_maps, n=NCORE, has_inv=None):
    for m in in_maps:
        m["c_ident"] = IDENT_NP
        m["c_sel8"] = SEL8_NP
        if has_inv:
            m["c_inv64"] = INV64_NP
    return run_bass_kernel_spmd(nc, in_maps, core_ids=list(range(n)))


def emit_wores(p, av_dram, E, wo_dram, bo_dram, x_dram, xout_dram, n_tok, deps=(),
               av_src=None):
    if av_src is None:
        av_src = lambda r0, r1, ec: av_dram[r0:r1, ec * 128:(ec + 1) * 128]
    EC = E // 128
    NSEG = n_tok // 512
    wscope = []
    NWOT = 2 if EC <= 16 else 1
    avt = [p.sbuf("wores_avt", [128, 512], BF16, wscope) for _ in range(EC)]
    wot = [[p.sbuf("wores_wo", [128, 512], BF16, wscope) for _ in range(EC)]
           for _ in range(NWOT)]
    borow = p.sbuf("wores_bo", [1, D], BF16, wscope)
    xs = [p.sbuf("wores_x", [128, D], F32, wscope) for _ in range(8)]
    ps = [p.psum("wores_ps", [128, 512], scope=wscope) for _ in range(4)]

    tbo = None
    if bo_dram is not None:
        tbo = p.dma(lambda e: e.dma_start(out=borow[0:1, :], in_=bo_dram[None, :]),
                    deps, q="gpsimd", key="wr_bo")

    wo_toks = [[None] * EC for _ in range(NWOT)]
    wo_used = [[None] * EC for _ in range(NWOT)]
    avt_toks = [None] * EC
    avt_used = [None] * EC
    ps_used = [None] * 4
    ps_mm = [None] * 4
    x_toks = [None] * 8
    x_used = [None] * 8
    store_toks = []

    for seg in range(NSEG):
        for ec in range(EC):
            dep = list(deps) + ([avt_used[ec]] if avt_used[ec] is not None else [])
            avt_toks[ec] = p.dma(lambda e, ec=ec, seg=seg: e.dma_start(
                out=avt[ec][:, :], in_=av_src(seg * 512, (seg + 1) * 512, ec),
                transpose=True), dep, q="scalar", key="wr_avt")
        for tt in range(4):
            g = seg % 2 * 4 + tt
            dep = list(deps) + ([x_used[g]] if x_used[g] is not None else [])
            x_toks[g] = p.dma(lambda e, seg=seg, tt=tt, g=g: e.dma_start(
                out=xs[g][:], in_=x_dram[seg * 512 + tt * 128:seg * 512 + (tt + 1) * 128, :]),
                dep, q="sync", key="wr_x")
        for ds in range(4):
            par = ds % NWOT
            for ec in range(EC):
                dep = [wo_used[par][ec]] if wo_used[par][ec] is not None else list(deps)
                wo_toks[par][ec] = p.dma(lambda e, ec=ec, ds=ds, par=par: e.dma_start(
                    out=wot[par][ec][:, :], in_=wo_dram[ec, ds, :, :]),
                    dep, q="gpsimd", key=("wr_wo", par))
            for tt in range(4):
                g = seg % 2 * 4 + tt
                bank = tt
                mmdep = ([ps_used[bank]] if ps_used[bank] is not None else []) + \
                    [avt_toks[ec2] for ec2 in range(EC)] + [wo_toks[par][ec2] for ec2 in range(EC)]
                if tbo is not None:
                    mmdep.append(tbo)
                def mk(e, par=par, tt=tt, ds=ds, bank=bank, has_bias=tbo is not None):
                    mm = None
                    for ec2 in range(EC):
                        mm = e.matmul(ps[bank][:, :], avt[ec2][:, tt * 128:(tt + 1) * 128],
                                      wot[par][ec2][:, :], start=(ec2 == 0), stop=False)
                    mm = e.matmul(ps[bank][:, :], p.ones_bf[0:1, :],
                                  (borow[0:1, ds * 512:(ds + 1) * 512] if has_bias
                                   else p.ones_bf[0:1, 0:512]),
                                  start=False, stop=True) if has_bias else None
                    if mm is None:
                        pass
                    return mm
                def mk2(e, par=par, tt=tt, ds=ds, bank=bank, has_bias=tbo is not None):
                    n = EC + (1 if has_bias else 0)
                    mm = None
                    for ec2 in range(EC):
                        mm = e.matmul(ps[bank][:, :], avt[ec2][:, tt * 128:(tt + 1) * 128],
                                      wot[par][ec2][:, :], start=(ec2 == 0),
                                      stop=(ec2 == EC - 1 and not has_bias))
                    if has_bias:
                        mm = e.matmul(ps[bank][:, :], p.ones_bf[0:1, :],
                                      borow[0:1, ds * 512:(ds + 1) * 512],
                                      start=False, stop=True)
                    return mm
                tmm = p.add("tensor", mk2, mmdep)
                ps_mm[bank] = tmm
                wo_used[par] = [tmm] * EC
                avt_used = [tmm] * EC
                tev = p.add("vector", lambda e, g=g, ds=ds, bank=bank: e.tensor_add(
                    xs[g][:, ds * 512:(ds + 1) * 512], ps[bank][:],
                    xs[g][:, ds * 512:(ds + 1) * 512]), [tmm, x_toks[g]])
                ps_used[bank] = tev
                x_used[g] = tev
        for tt in range(4):
            g = seg % 2 * 4 + tt
            ts = p.dma(lambda e, seg=seg, tt=tt, g=g: e.dma_start(
                out=xout_dram[seg * 512 + tt * 128:seg * 512 + (tt + 1) * 128, :],
                in_=xs[g][:]), [x_used[g]], q="sync", key="wr_st")
            x_used[g] = ts
            store_toks.append(ts)
    p.fence()
    p.release(wscope)
    return store_toks[-1]


def build_wores_launch(E, has_bias, ser=False):
    p = Prog()
    av = p.dram("av", [T, E], BF16, kind="ExternalInput")
    wo = p.dram("wo", [E, D], F32, kind="ExternalInput")
    bo = p.dram("bo", [D], F32, kind="ExternalInput") if has_bias else None
    x = p.dram("x", [T, D], F32, kind="ExternalInput")
    xout = p.dram("xout", [T, D], F32, kind="ExternalOutput")
    emit_wores(p, av, E, wo, bo, x, xout, T)
    return p.build(debug_serialize=ser)


def emit_quantize(p, w_dram, R, C, wq_dram, thr_dram, deps=()):
    scope = []
    NR = R // 128
    NCC = C
    wt = [p.sbuf("qz_w", [128, C], F32, scope) for _ in range(2)]
    acc = p.sbuf("qz_acc", [128, 2], F32, scope)
    part = p.sbuf("qz_p", [128, 1], F32, scope)
    rowv = p.sbuf("qz_row", [1, 128], F32, scope)
    tot = p.sbuf("qz_tot", [1, 1], F32, scope)
    thrb = p.sbuf("qz_thrb", [128, 1], F32, scope)
    thrn = p.sbuf("qz_thrn", [128, 1], F32, scope)
    sg = p.sbuf("qz_sg", [128, C], F32, scope)
    aw = p.sbuf("qz_aw", [128, C], F32, scope)
    mk = p.sbuf("qz_mk", [128, C], F32, scope)
    wq = [p.sbuf("qz_wq", [128, C], BF16, scope) for _ in range(2)]
    acc_dram = p.dram(p.name("qz_accd"), [128, 1], F32)

    tz = p.add("vector", lambda e: e.memset(acc[:, 0:1], 0.0), deps, small=True)
    ld_tok = [None, None]
    use_tok = [None, None]
    tacc = tz
    for i in range(NR):
        k = i % 2
        tl = p.dma(lambda e, i=i, k=k: e.dma_start(
            out=wt[k][:], in_=w_dram[i * 128:(i + 1) * 128, :]),
            list(deps) + ([use_tok[k]] if use_tok[k] else []), key=("qz_w", k))
        tr = p.add("vector", lambda e, k=k: e.reduce_sum(
            part[:, 0:1], wt[k][:], axis=AX.X, apply_absolute_value=True),
            [tl, tacc], small=True)
        tacc = p.add("vector", lambda e: e.tensor_add(
            acc[:, 0:1], acc[:, 0:1], part[:, 0:1]), [tr], small=True)
        use_tok[k] = tr
    t1 = p.dma(lambda e: e.dma_start(out=acc_dram[:, :], in_=acc[:, 0:1]),
               [tacc], key="qz_s")
    t2 = p.dma(lambda e: e.dma_start(
        out=rowv[0:1, :], in_=acc_dram.rearrange("a b -> b a")), [t1], key="qz_s")
    t3 = p.add("vector", lambda e: e.reduce_sum(tot[0:1, 0:1], rowv[0:1, :], axis=AX.X),
               [t2], small=True)
    t4 = p.add("vector", lambda e: e.tensor_scalar(
        tot[0:1, 0:1], tot[0:1, 0:1], 0.5 / (R * C), 0.5e-8, ALU.mult, ALU.add),
        [t3], small=True)
    t5 = p.dma(lambda e: e.dma_start(out=thr_dram[0:1, 0:1], in_=tot[0:1, 0:1]),
               [t4], key="qz_s")
    bc_ap = bass.AP(tensor=thr_dram.tensor, offset=0, ap=[[0, 128], [1, 1]])
    t6 = p.dma(lambda e: e.dma_start(out=thrb[:, 0:1], in_=bc_ap,
                                     allow_slow_non_contiguous=True),
               [t5], key="qz_s")
    t7 = p.add("vector", lambda e: e.tensor_scalar_mul(thrn[:, 0:1], thrb[:, 0:1], -1.0),
               [t6], small=True)
    use_tok = [None, None]
    st_tok = [None, None]
    prev_tq = None
    last = None
    for i in range(NR):
        k = i % 2
        tl = p.dma(lambda e, i=i, k=k: e.dma_start(
            out=wt[k][:], in_=w_dram[i * 128:(i + 1) * 128, :]),
            [t6] + ([use_tok[k]] if use_tok[k] else []), key=("qz_w", k))
        tpos = p.add("vector", lambda e, k=k: e.tensor_single_scalar(
            sg[:], wt[k][:], thrb[:, 0:1], ALU.is_ge),
            [tl, t6] + ([prev_tq] if prev_tq else []))
        tneg = p.add("vector", lambda e, k=k: e.tensor_single_scalar(
            mk[:], wt[k][:], thrn[:, 0:1], ALU.is_le),
            [tl, t7] + ([prev_tq] if prev_tq else []))
        tq = p.add("vector", lambda e, k=k: e.tensor_sub(wq[k][:], sg[:], mk[:]),
                   [tpos, tneg])
        prev_tq = tq
        use_tok[k] = tq
        ts = p.dma(lambda e, i=i, k=k: e.dma_start(
            out=wq_dram[i * 128:(i + 1) * 128, :], in_=wq[k][:]), [tq], key="qz_st")
        st_tok[k] = ts
        last = ts
    fence = p.fence()
    p.release(scope)
    return last, fence


def emit_bitffn(p, xn_dram, w1q, w2q, w3q, hidden_dram, x_dram, xout_dram,
                n_tok, deps=()):
    H = 5632
    NFS = H // 512
    NTS = n_tok // 512
    scopeA = []
    xnf = [p.sbuf("bf_xnf", [128, n_tok], BF16, scopeA) for _ in range(16)]
    w1t = [[p.sbuf("bf_w1t", [128, 512], BF16, scopeA) for _ in range(16)] for _ in range(2)]
    w2t = [[p.sbuf("bf_w2t", [128, 512], BF16, scopeA) for _ in range(16)] for _ in range(2)]
    sgt = p.sbuf("bf_sg", [128, 512], F32, scopeA)
    hid = [p.sbuf("bf_hid", [128, 512], BF16, scopeA) for _ in range(2)]
    ps1 = p.psum("bf_ps1", [128, 512], scope=scopeA)
    ps2 = p.psum("bf_ps2", [128, 512], scope=scopeA)

    xnf_toks = []
    for dc in range(16):
        xnf_toks.append(p.dma(lambda e, dc=dc: e.dma_start(
            out=xnf[dc][:, :], in_=xn_dram[0:n_tok, dc * 128:(dc + 1) * 128],
            transpose=True), deps, q="scalar", key="bf_xnf"))

    wld = [[None, None], [None, None]]
    wuse = [[None, None], [None, None]]
    ps1_use = None
    ps2_use = None
    hid_use = [None, None]
    for fs in range(NFS):
        par = fs % 2
        l1 = []
        l2 = []
        for dc in range(16):
            d1 = [wuse[0][par]] if wuse[0][par] else list(deps)
            l1.append(p.dma(lambda e, fs=fs, dc=dc, par=par: e.dma_start(
                out=w1t[par][dc][:, :],
                in_=w1q[fs, dc, :, :]), d1, q="scalar", key=("bf_w1", par)))
            d2 = [wuse[1][par]] if wuse[1][par] else list(deps)
            l2.append(p.dma(lambda e, fs=fs, dc=dc, par=par: e.dma_start(
                out=w2t[par][dc][:, :],
                in_=w2q[fs, dc, :, :]), d2, q="scalar", key=("bf_w2", par)))
        for fb in range(4):
            for ts in range(NTS):
                hk = (fb * NTS + ts) % 2
                m1 = p.add("tensor", lambda e, par=par, fb=fb, ts=ts: [
                    e.matmul(ps1[:, :], w1t[par][dc2][:, fb * 128:(fb + 1) * 128],
                             xnf[dc2][:, ts * 512:(ts + 1) * 512],
                             start=(dc2 == 0), stop=(dc2 == 15))
                    for dc2 in range(16)][-1],
                    l1 + xnf_toks + ([ps1_use] if ps1_use else []))
                m2 = p.add("tensor", lambda e, par=par, fb=fb, ts=ts: [
                    e.matmul(ps2[:, :], w2t[par][dc2][:, fb * 128:(fb + 1) * 128],
                             xnf[dc2][:, ts * 512:(ts + 1) * 512],
                             start=(dc2 == 0), stop=(dc2 == 15))
                    for dc2 in range(16)][-1],
                    l2 + xnf_toks + ([ps2_use] if ps2_use else []))
                tsg = p.add("scalar", lambda e: e.activation(
                    sgt[:], ps1[:], AF.Sigmoid), [m1])
                ps1_use = tsg
                th = p.add("vector", lambda e, hk=hk: e.tensor_mul(
                    hid[hk][:], sgt[:], ps2[:]), [tsg, m2] +
                    ([hid_use[hk]] if hid_use[hk] else []))
                ps2_use = th
                tst = p.dma(lambda e, fs=fs, fb=fb, ts=ts, hk=hk: e.dma_start(
                    out=hidden_dram[fs * 512 + fb * 128:fs * 512 + (fb + 1) * 128,
                                    ts * 512:(ts + 1) * 512], in_=hid[hk][:]),
                    [th], key="bf_hst")
                hid_use[hk] = tst
        wuse[0][par] = p.last("tensor")
        wuse[1][par] = wuse[0][par]
    fenceA = p.fence()
    p.release(scopeA)

    scopeB = []
    NH = H // 128
    hall = [p.sbuf("bf_hall", [128, 512], BF16, scopeB) for _ in range(NH)]
    w3t = [p.sbuf("bf_w3t", [128, 512], BF16, scopeB) for _ in range(NH)]
    xs = [p.sbuf("bf_x", [128, D], F32, scopeB) for _ in range(4)]
    xo = [p.sbuf("bf_xo", [128, D], F32, scopeB) for _ in range(4)]
    psb = [p.psum("bf_psb", [128, 512], scope=scopeB) for _ in range(4)]

    hall_use = [None] * NH
    w3_use = [None] * NH
    x_use = [None] * 4
    psb_use = [None] * 4
    last = None
    for tg in range(NTS):
        htoks = []
        for fc in range(NH):
            dep = fenceA + ([hall_use[fc]] if hall_use[fc] else [])
            htoks.append(p.dma(lambda e, tg=tg, fc=fc: e.dma_start(
                out=hall[fc][:, :],
                in_=hidden_dram[fc * 128:(fc + 1) * 128, tg * 512:(tg + 1) * 512]),
                dep, key="bf_hld"))
        xtoks = []
        for tt in range(4):
            dep = fenceA + ([x_use[tt]] if x_use[tt] else [])
            xtoks.append(p.dma(lambda e, tg=tg, tt=tt: e.dma_start(
                out=xs[tt][:], in_=x_dram[tg * 512 + tt * 128:tg * 512 + (tt + 1) * 128, :]),
                dep, key="bf_x"))
        for ds in range(4):
            w3toks = []
            for fc in range(NH):
                dep = fenceA + ([w3_use[fc]] if w3_use[fc] else [])
                w3toks.append(p.dma(lambda e, ds=ds, fc=fc: e.dma_start(
                    out=w3t[fc][:, :],
                    in_=w3q[ds, fc, :, :]), dep, q="scalar", key="bf_w3"))
            for tt in range(4):
                bank = tt
                mm = p.add("tensor", lambda e, tt=tt, bank=bank: [
                    e.matmul(psb[bank][:, :], hall[fc2][:, tt * 128:(tt + 1) * 128],
                             w3t[fc2][:, :], start=(fc2 == 0), stop=(fc2 == NH - 1))
                    for fc2 in range(NH)][-1],
                    htoks + w3toks + ([psb_use[bank]] if psb_use[bank] else []))
                w3_use = [mm] * NH
                hall_use = [mm] * NH
                tev = p.add("vector", lambda e, tt=tt, ds=ds, bank=bank: e.tensor_add(
                    xo[tt][:, ds * 512:(ds + 1) * 512], psb[bank][:],
                    xs[tt][:, ds * 512:(ds + 1) * 512]), [mm, xtoks[tt]])
                psb_use[bank] = tev
                x_use[tt] = tev
        for tt in range(4):
            ts2 = p.dma(lambda e, tg=tg, tt=tt: e.dma_start(
                out=xout_dram[tg * 512 + tt * 128:tg * 512 + (tt + 1) * 128, :],
                in_=xo[tt][:]), [x_use[tt]], key="bf_st")
            x_use[tt] = ts2
            last = ts2
    fenceB = p.fence()
    p.release(scopeB)
    return last, fenceB


def build_ffn_test_launch():
    p = Prog()
    H = 5632
    xn = p.dram("xn", [T, D], BF16, kind="ExternalInput")
    w1 = p.dram("w1", [H, D], F32, kind="ExternalInput")
    w2 = p.dram("w2", [H, D], F32, kind="ExternalInput")
    w3 = p.dram("w3", [D, H], F32, kind="ExternalInput")
    x = p.dram("x", [T, D], F32, kind="ExternalInput")
    xout = p.dram("xout", [T, D], F32, kind="ExternalOutput")
    w1q = p.dram("w1q", [H, D], BF16)
    w2q = p.dram("w2q", [H, D], BF16)
    w3q = p.dram("w3q", [D, H], BF16)
    thr = p.dram("thr", [1, 1], F32)
    hidden = p.dram("hidden", [H, T], BF16)
    _, f1 = emit_quantize(p, w1, H, D, w1q, thr)
    _, f2 = emit_quantize(p, w2, H, D, w2q, thr)
    _, f3 = emit_quantize(p, w3, D, H, w3q, thr)
    emit_bitffn(p, xn, w1q, w2q, w3q, hidden, x, xout, T, deps=f1 + f2 + f3)
    return p.build()


def emit_rope_tables(p, cos_dram, sin_dram, n_pos, deps=()):
    scope = []
    inv_row = p.sbuf("rt_inv", [1, 64], F32, scope)
    jrow = p.sbuf("rt_j", [1, 64], I32, scope)
    jrowf = p.sbuf("rt_jf", [1, 64], F32, scope)
    inv_bc = p.sbuf("rt_invbc", [128, 64], F32, scope)
    tcol = p.sbuf("rt_t", [128, 1], I32, scope)
    tcolf = p.sbuf("rt_tf", [128, 1], F32, scope)
    ang = p.sbuf("rt_ang", [128, 64], F32, scope)
    red = p.sbuf("rt_red", [128, 64], F32, scope)
    ni = p.sbuf("rt_ni", [128, 64], I32, scope)
    nf = p.sbuf("rt_nf", [128, 64], F32, scope)
    outc = p.sbuf("rt_c", [128, 64], F32, scope)
    outs = p.sbuf("rt_s", [128, 64], F32, scope)
    inv_dram = p.dram(p.name("rt_invd"), [1, 64], F32)
    PI = float(np.pi)
    TWO_PI = 2 * PI
    p.want_const(-PI, PI / 2 - PI)

    inv_in = p.dram("c_inv64", [1, 64], F32, kind="ExternalInput")
    bc_ap = bass.AP(tensor=inv_in.tensor, offset=0, ap=[[0, 128], [1, 64]])
    tbc = p.dma(lambda e: e.dma_start(out=inv_bc[:, :], in_=bc_ap,
                                      allow_slow_non_contiguous=True), deps,
                key="rt_s", q="gpsimd")

    last = None
    prev = [None, None]
    for it in range(n_pos // 128):
        tt = p.add("gpsimd", lambda e, it=it: e.iota(
            tcol[:, 0:1], pattern=[[0, 1]], base=it * 128, channel_multiplier=1),
            [tbc] + ([prev[0]] if prev[0] else []), small=True)
        ttf = p.add("vector", lambda e: e.tensor_copy(tcolf[:, 0:1], tcol[:, 0:1]),
                    [tt], small=True)
        tang = p.add("vector", lambda e: e.tensor_single_scalar(
            ang[:, :], inv_bc[:, :], tcolf[:, 0:1], ALU.mult), [ttf, tbc], small=True)
        C1 = float(np.float32(6.283185307179586).view(np.uint32) & 0xFFFFF000
                   ) if False else None
        c1 = float(np.frombuffer(
            (np.float32(TWO_PI).view(np.uint32) & np.uint32(0xFFFFF000)
             ).tobytes(), dtype=np.float32)[0])
        c2f = np.float32(TWO_PI - c1)
        c2 = float(np.frombuffer(
            (c2f.view(np.uint32) & np.uint32(0xFFFFF000)).tobytes(),
            dtype=np.float32)[0])
        c3 = float(TWO_PI - c1 - c2)
        prev_sin = None
        for which, (dst, dram, shift) in enumerate(
                ((outc, cos_dram, PI / 2), (outs, sin_dram, 0.0))):
            sh0 = shift
            t1 = p.add("vector", lambda e, sh0=sh0: e.tensor_scalar(
                red[:, :], ang[:, :], 1.0 / TWO_PI, sh0 / TWO_PI, ALU.mult, ALU.add),
                [tang] + ([prev_sin] if prev_sin else []), small=True)
            t2 = p.add("vector", lambda e: e.tensor_copy(ni[:, :], red[:, :]),
                       [t1], small=True)
            t3 = p.add("vector", lambda e: e.tensor_copy(nf[:, :], ni[:, :]),
                       [t2], small=True)
            t4 = p.add("vector", lambda e: e.scalar_tensor_tensor(
                red[:, :], nf[:, :], -c1, ang[:, :], ALU.mult, ALU.add),
                [t3], small=True)
            t4b = p.add("vector", lambda e: e.scalar_tensor_tensor(
                red[:, :], nf[:, :], -c2, red[:, :], ALU.mult, ALU.add),
                [t4], small=True)
            t4c = p.add("vector", lambda e: e.scalar_tensor_tensor(
                red[:, :], nf[:, :], -c3, red[:, :], ALU.mult, ALU.add),
                [t4b], small=True)
            t5 = p.add("vector", lambda e, sh0=sh0: e.tensor_scalar_add(
                red[:, :], red[:, :], sh0), [t4c], small=True)
            t6 = p.add("scalar", lambda e, dst=dst: e.activation(
                dst[:, :], red[:, :], AF.Sin), [t5], small=True)
            prev_sin = t6
            tst = p.dma(lambda e, it=it, dst=dst, dram=dram: e.dma_start(
                out=dram[it * 128:(it + 1) * 128, :], in_=dst[:, :]), [t6],
                key="rt_st")
            prev[which] = tst
            last = tst
    fence = p.fence()
    p.release(scope)
    return last, fence


def emit_proj(p, xn_dram, w_dram, b_dram, out_dram, n_tok, C, epilogue=None,
              deps=(), tag="pj"):
    HALF = 1024 if n_tok >= 2048 else n_tok
    NH = n_tok // HALF
    NCS = ceil_div(C, 512)
    scope = []
    xnf = [p.sbuf(tag + "_xnf", [128, HALF], BF16, scope) for _ in range(16)]
    wt = [[p.sbuf(tag + "_wt", [128, 512], BF16, scope) for _ in range(16)]
          for _ in range(2)]
    brow = p.sbuf(tag + "_b", [1, C], BF16, scope)
    ot = [p.sbuf(tag + "_ot", [128, 512], BF16, scope) for _ in range(2)]
    ps = [p.psum(tag + "_ps", [128, 512], scope=scope) for _ in range(2)]

    tbo = None
    if b_dram is not None:
        tbo = p.dma(lambda e: e.dma_start(out=brow[0:1, 0:C], in_=b_dram[None, :]),
                    deps, q="gpsimd", key=tag + "_b")
    w_use = [None, None]
    xnf_use = [None] * 16
    ps_use = [None, None]
    ot_use = [None, None]
    last = None
    for h in range(NH):
        xnf_toks = []
        for dc in range(16):
            dep = list(deps) + ([xnf_use[dc]] if xnf_use[dc] else [])
            xnf_toks.append(p.dma(lambda e, h=h, dc=dc: e.dma_start(
                out=xnf[dc][:, 0:HALF],
                in_=xn_dram[h * HALF:(h + 1) * HALF, dc * 128:(dc + 1) * 128],
                transpose=True), dep, q="scalar", key=tag + "_xnf"))
        for cs in range(NCS):
            par = cs % 2
            cw = min(512, C - cs * 512)
            wtoks = []
            for dc in range(16):
                dep = [w_use[par]] if w_use[par] else list(deps)
                wtoks.append(p.dma(lambda e, cs=cs, dc=dc, par=par, cw=cw: e.dma_start(
                    out=wt[par][dc][:, 0:cw], in_=w_dram[dc, cs, :, 0:cw]),
                    dep, q="gpsimd", key=(tag + "_w", par)))
            for tt in range(HALF // 128):
                pk = tt % 2
                def mk(e, par=par, tt=tt, cs=cs, cw=cw, pk=pk, has_b=tbo is not None):
                    mm = None
                    for dc2 in range(16):
                        mm = e.matmul(ps[pk][:, 0:cw], xnf[dc2][:, tt * 128:(tt + 1) * 128],
                                      wt[par][dc2][:, 0:cw], start=(dc2 == 0),
                                      stop=(dc2 == 15 and not has_b))
                    if has_b:
                        mm = e.matmul(ps[pk][:, 0:cw], p.ones_bf[0:1, :],
                                      brow[0:1, cs * 512:cs * 512 + cw],
                                      start=False, stop=True)
                    return mm
                mmdep = xnf_toks + wtoks + ([ps_use[pk]] if ps_use[pk] else []) + \
                    ([tbo] if tbo else [])
                tmm = p.add("tensor", mk, mmdep)
                w_use[par] = tmm
                xnf_use = [tmm] * 16
                ti = h * (HALF // 128) + tt
                epdeps = [tmm] + ([ot_use[pk]] if ot_use[pk] else [])
                if epilogue is None:
                    tep = p.add("vector", lambda e, pk=pk, cw=cw: e.tensor_copy(
                        ot[pk][:, 0:cw], ps[pk][:, 0:cw]), epdeps)
                else:
                    tep = epilogue(p, ps[pk], ot[pk], ti, cs, cw, epdeps)
                ps_use[pk] = tep
                tst = p.dma(lambda e, ti=ti, cs=cs, cw=cw, pk=pk: e.dma_start(
                    out=out_dram[ti * 128:(ti + 1) * 128, cs * 512:cs * 512 + cw],
                    in_=ot[pk][:, 0:cw]), [tep], key=tag + "_st")
                ot_use[pk] = tst
                last = tst
    fence = p.fence()
    p.release(scope)
    return last, fence


def emit_flash(p, q_dram, k_dram, v_dram, av_dram, n_heads, n_tok, scale,
               mask_dram, deps=(), tag="fl"):
    NQB = n_tok // 512
    scope = []
    qf = p.sbuf(tag + "_qf", [128, n_tok], BF16, scope)
    kf = p.sbuf(tag + "_kf", [128, n_tok], BF16, scope)
    vt = [p.sbuf(tag + "_vt", [128, 129], BF16, scope) for _ in range(n_tok // 128)]
    pt = [p.sbuf(tag + "_pt", [128, 512], BF16, scope) for _ in range(4)]
    mask = p.sbuf(tag + "_mk", [128, 128], BF16, scope)
    nshift = p.sbuf(tag + "_ns", [128, 1], F32, scope)
    rcol4 = [p.sbuf(tag + "_r", [128, 1], F32, scope) for _ in range(4)]
    avt = [p.sbuf(tag + "_av", [128, 128], BF16, scope) for _ in range(2)]
    st_ps = [p.psum(tag + "_st", [128, 512], scope=scope) for _ in range(4)]
    acc_ps = [p.psum(tag + "_acc", [128, 512], scope=scope) for _ in range(4)]
    p.want_const(-SOFTMAX_SHIFT)

    tmk = p.dma(lambda e: e.dma_start(out=mask[:, :], in_=mask_dram[:, :]),
                deps, q="gpsimd", key=tag + "_mk")
    tns = p.add("vector", lambda e: e.memset(nshift[:, 0:1], -SOFTMAX_SHIFT),
                deps, small=True)
    ones_toks = []
    for kc in range(n_tok // 128):
        ones_toks.append(p.add(
            "vector", lambda e, kc=kc: e.memset(vt[kc][:, 128:129], 1.0),
            deps, small=True))

    qkv_use = []
    last = None
    for h in range(n_heads):
        dep_h = list(deps) + qkv_use
        tq = p.dma(lambda e, h=h: e.dma_start(
            out=qf[:, 0:n_tok], in_=q_dram[0:n_tok, h * 128:(h + 1) * 128],
            transpose=True), dep_h, q="scalar", key=tag + "_q")
        tk = p.dma(lambda e, h=h: e.dma_start(
            out=kf[:, 0:n_tok], in_=k_dram[0:n_tok, h * 128:(h + 1) * 128],
            transpose=True), dep_h, q="scalar", key=tag + "_k")
        vtoks = []
        for kc in range(n_tok // 128):
            vtoks.append(p.dma(lambda e, h=h, kc=kc: e.dma_start(
                out=vt[kc][:, 0:128],
                in_=v_dram[kc * 128:(kc + 1) * 128, h * 128:(h + 1) * 128]),
                dep_h + [ones_toks[kc]], q="sync", key=tag + "_v"))
        qkv_use = []
        acc_free = [None] * 4
        av_store = [None, None]
        for qb in range(NQB):
            acc_mm = [None] * 4
            st_use = [None] * 4
            pt_use = [None] * 4
            first_av = [True] * 4
            for kb in range(qb + 1):
                diag = (kb == qb)
                for kc in range(4):
                    kg = kb * 4 + kc
                    tst = p.add("tensor", lambda e, kg=kg, qb=qb, kc=kc: e.matmul(
                        st_ps[kc][:, :], kf[:, kg * 128:(kg + 1) * 128],
                        qf[:, qb * 512:(qb + 1) * 512], start=True, stop=True),
                        [tq, tk] + ([st_use[kc]] if st_use[kc] else []))
                    texp = p.add("scalar", lambda e, kc=kc: e.activation(
                        pt[kc][:, :], st_ps[kc][:, :], AF.Exp,
                        bias=nshift[:, 0:1], scale=scale),
                        [tst, tns] + ([pt_use[kc]] if pt_use[kc] else []))
                    st_use[kc] = texp
                    if diag:
                        texp = p.add("vector", lambda e, kc=kc: e.tensor_mul(
                            pt[kc][:, kc * 128:(kc + 1) * 128],
                            pt[kc][:, kc * 128:(kc + 1) * 128], mask[:, :]),
                            [texp, tmk])
                    pt_use[kc] = texp
                for qt in range(4):
                    kc_hi = (qt + 1) if diag else 4
                    for kc in range(kc_hi):
                        kg = kb * 4 + kc
                        is_first = first_av[qt]
                        first_av[qt] = False
                        is_last = (kb == qb) and (kc == kc_hi - 1)
                        tav = p.add("tensor", lambda e, kg=kg, qt=qt, kc=kc,
                                    is_first=is_first, is_last=is_last: e.matmul(
                            acc_ps[qt][:, 0:129], pt[kc][:, qt * 128:(qt + 1) * 128],
                            vt[kg][:, :], start=is_first, stop=is_last),
                            [pt_use[kc], vtoks[kg], acc_mm[qt]] +
                            ([acc_free[qt]] if is_first else []))
                        pt_use[kc] = tav
                        acc_mm[qt] = tav
            for qt in range(4):
                ak = qt % 2
                trc = p.add("vector", lambda e, qt=qt: e.reciprocal(
                    rcol4[qt][:, 0:1], acc_ps[qt][:, 128:129]), [acc_mm[qt]], small=True)
                tavo = p.add("scalar", lambda e, qt=qt, ak=ak: e.activation(
                    avt[ak][:, :], acc_ps[qt][:, 0:128], AF.Copy, scale=rcol4[qt][:, 0:1]),
                    [trc] + ([av_store[ak]] if av_store[ak] else []))
                tsa = p.dma(lambda e, h=h, qb=qb, qt=qt, ak=ak: e.dma_start(
                    out=av_dram[qb * 512 + qt * 128:qb * 512 + (qt + 1) * 128,
                                h * 128:(h + 1) * 128], in_=avt[ak][:, :]),
                    [tavo], key=tag + "_avst")
                last = tsa
                av_store[ak] = tsa
                acc_free[qt] = tavo
        qkv_use = [p.last("tensor"), p.last("vector")]
    fence = p.fence()
    p.release(scope)
    return last, fence


def emit_cast(p, src_dram, dst_dram, rows, cols, deps=(), tag="ct", chunk=512):
    toks = []
    for r0 in range(0, rows, chunk):
        r1 = min(rows, r0 + chunk)
        toks.append(p.dma(lambda e, r0=r0, r1=r1: e.dma_start(
            out=dst_dram[r0:r1, :], in_=src_dram[r0:r1, :]), deps,
            q="gpsimd", key=tag))
    return toks


def emit_moe(p, xn_dram, gwt_bf, gate_bf, up_bf, down_bf, shg_bf, shu_bf, shd_bf,
             hidden_dram, x_dram, xout_dram, n_tok, deps=(), tag="moe", dbg=None,
             dbg2=None, w3key=None):
    NT128 = n_tok // 128
    NTS = n_tok // 512
    F, FS = 1408, 2816
    NEXP = 8
    BIG = 1.0e9

    scopeR = []
    scope0 = []
    xnf = [p.sbuf(tag + "_xnf", [128, n_tok], BF16, scope0) for _ in range(16)]
    wmatT = p.sbuf(tag + "_wmT", [8, n_tok], BF16, scope0)
    gwt = [p.sbuf(tag + "_gwt", [128, 8], BF16, scopeR) for _ in range(16)]
    lg = p.sbuf(tag + "_lg", [128, 8], F32, scopeR)
    mx = p.sbuf(tag + "_mx", [128, 1], F32, scopeR)
    pexp = p.sbuf(tag + "_pe", [128, 8], F32, scopeR)
    lsum = p.sbuf(tag + "_ls", [128, 1], F32, scopeR)
    rr = p.sbuf(tag + "_rr", [128, 1], F32, scopeR)
    sc = p.sbuf(tag + "_sc", [128, 8], F32, scopeR)
    gg = p.sbuf(tag + "_gg", [128, 4], F32, scopeR)
    g1 = p.sbuf(tag + "_g1", [128, 1], F32, scopeR)
    msk1 = p.sbuf(tag + "_m1", [128, 4], F32, scopeR)
    gm = p.sbuf(tag + "_gm", [128, 4], F32, scopeR)
    g2 = p.sbuf(tag + "_g2", [128, 1], F32, scopeR)
    gmask = p.sbuf(tag + "_gk", [128, 4], F32, scopeR)
    smask = p.sbuf(tag + "_sk", [128, 8], F32, scopeR)
    mskd = p.sbuf(tag + "_md", [128, 8], F32, scopeR)
    wmat = p.sbuf(tag + "_wm", [128, 8], BF16, scopeR)
    tmp8 = p.sbuf(tag + "_t8", [128, 8], F32, scopeR)
    dbgt = p.sbuf(tag + "_dbgt", [128, 48], F32, scopeR)
    ps_lg = p.psum(tag + "_plg", [128, 512], scope=scopeR)
    ps_tr = p.psum(tag + "_ptr", [128, 128], scope=scopeR)


    xnf_toks = []
    for dc in range(16):
        xnf_toks.append(p.dma(lambda e, dc=dc: e.dma_start(
            out=xnf[dc][:, 0:n_tok], in_=xn_dram[0:n_tok, dc * 128:(dc + 1) * 128],
            transpose=True), deps, q="scalar", key=tag + "_xnf"))
    gw_toks = []
    for dc in range(16):
        gw_toks.append(p.dma(lambda e, dc=dc: e.dma_start(
            out=gwt[dc][:, :], in_=gwt_bf[dc * 128:(dc + 1) * 128, 0:8]),
            deps, q="sync", key=tag + "_gw"))

    prev_cycle = None
    trT_last = None
    for ti in range(NT128):
        dep0 = xnf_toks + gw_toks + ([prev_cycle] if prev_cycle else [])
        tmm = p.add("tensor", lambda e, ti=ti: [
            e.matmul(ps_lg[:, 0:8], xnf[dc2][:, ti * 128:(ti + 1) * 128],
                     gwt[dc2][:, :], start=(dc2 == 0), stop=(dc2 == 15))
            for dc2 in range(16)][-1], dep0)
        t0 = p.add("vector", lambda e: e.tensor_copy(lg[:, :], ps_lg[:, 0:8]),
                   [tmm], small=True)
        t1 = p.add("vector", lambda e: e.reduce_max(mx[:, 0:1], lg[:, :], axis=AX.X),
                   [t0], small=True)
        t1b = p.add("vector", lambda e: e.tensor_scalar_mul(mx[:, 0:1], mx[:, 0:1],
                                                            -1.0), [t1], small=True)
        t2 = p.add("scalar", lambda e: e.activation(pexp[:, :], lg[:, :], AF.Exp,
                                                    bias=mx[:, 0:1],
                                                    accum_out=lsum[:, 0:1]),
                   [t1b], small=True)
        t3 = p.add("vector", lambda e: e.reciprocal(rr[:, 0:1], lsum[:, 0:1]),
                   [t2], small=True)
        t4 = p.add("vector", lambda e: e.tensor_single_scalar(
            sc[:, :], pexp[:, :], rr[:, 0:1], ALU.mult), [t3], small=True)
        t5 = p.add("vector", lambda e: e.tensor_tensor(
            gg[:, :], sc[:, 0:8:2], sc[:, 1:8:2], ALU.max), [t4], small=True)
        t6 = p.add("vector", lambda e: e.reduce_max(g1[:, 0:1], gg[:, :], axis=AX.X),
                   [t5], small=True)
        t7 = p.add("vector", lambda e: e.tensor_single_scalar(
            msk1[:, :], gg[:, :], g1[:, 0:1], ALU.is_ge), [t6], small=True)
        t8 = p.add("vector", lambda e: e.scalar_tensor_tensor(
            gm[:, :], msk1[:, :], -BIG, gg[:, :], ALU.mult, ALU.add), [t7], small=True)
        t9 = p.add("vector", lambda e: e.reduce_max(g2[:, 0:1], gm[:, :], axis=AX.X),
                   [t8], small=True)
        t10 = p.add("vector", lambda e: e.tensor_single_scalar(
            gmask[:, :], gg[:, :], g2[:, 0:1], ALU.is_ge), [t9], small=True)
        t11a = p.add("vector", lambda e: e.tensor_copy(smask[:, 0:8:2], gmask[:, :]),
                     [t10], small=True)
        t11b = p.add("vector", lambda e: e.tensor_copy(smask[:, 1:8:2], gmask[:, :]),
                     [t10], small=True)
        t12 = p.add("vector", lambda e: e.tensor_mul(mskd[:, :], sc[:, :], smask[:, :]),
                    [t11a, t11b], small=True)
        t13 = p.add("vector", lambda e: e.reduce_max(g1[:, 0:1], mskd[:, :], axis=AX.X),
                    [t12], small=True)
        t14 = p.add("vector", lambda e: e.tensor_single_scalar(
            smask[:, :], mskd[:, :], g1[:, 0:1], ALU.is_ge), [t13], small=True)
        t15 = p.add("vector", lambda e: e.scalar_tensor_tensor(
            tmp8[:, :], smask[:, :], -BIG, mskd[:, :], ALU.mult, ALU.add),
            [t14], small=True)
        t16 = p.add("vector", lambda e: e.reduce_max(g2[:, 0:1], tmp8[:, :], axis=AX.X),
                    [t15], small=True)
        t17 = p.add("vector", lambda e: e.tensor_single_scalar(
            smask[:, :], mskd[:, :], g2[:, 0:1], ALU.is_ge), [t16], small=True)
        t18 = p.add("vector", lambda e: e.tensor_mul(wmat[:, :], mskd[:, :],
                                                     smask[:, :]), [t17], small=True)
        if dbg2 is not None and ti == 0:
            for col, src in ((0, lg), (8, sc), (16, gg), (20, gmask), (24, mskd),
                             (32, smask), (40, wmat)):
                p.add("vector", lambda e, col=col, src=src: e.tensor_copy(
                    dbgt[:, col:col + src.shape[-1] if False else col + (4 if src is gg or src is gmask else 8)][:, 0:(4 if src is gg or src is gmask else 8)],
                    src[:, :]), [t18], small=True)
        ttr = p.add("tensor", lambda e: e.matmul(ps_tr[0:8, 0:128], wmat[:, :],
                                                 p.ident_bf[:, :], start=True,
                                                 stop=True),
                    [t18] + ([trT_last] if trT_last else []))
        tcp = p.add("vector", lambda e, ti=ti: e.tensor_copy(
            wmatT[0:8, ti * 128:(ti + 1) * 128], ps_tr[0:8, 0:128]), [ttr], small=True)
        trT_last = tcp
        prev_cycle = tcp
    if dbg is not None:
        p.dma(lambda e: e.dma_start(out=dbg[0:8, 0:n_tok], in_=wmatT[0:8, 0:n_tok]),
              [trT_last], key=tag + "_dbg")
    if dbg2 is not None:
        p.dma(lambda e: e.dma_start(out=dbg2[0:128, :], in_=dbgt[:, :]),
              [p.last("vector")], key=tag + "_dbg2")
    fenceR = p.fence()
    p.release(scopeR)

    scopeH = []
    w1t = [[p.sbuf(tag + "_w1t", [128, 512], BF16, scopeH) for _ in range(16)]
           for _ in range(2)]
    w2t = [[p.sbuf(tag + "_w2t", [128, 512], BF16, scopeH) for _ in range(16)]
           for _ in range(2)]
    sgt = p.sbuf(tag + "_sg", [128, 512], F32, scopeH)
    tmu = p.sbuf(tag + "_tm", [128, 512], F32, scopeH)
    hid = [p.sbuf(tag + "_hid", [128, 512], BF16, scopeH) for _ in range(2)]
    wbc = p.sbuf(tag + "_wbc", [128, n_tok], F32, scopeH)
    wrow = p.sbuf(tag + "_wrow", [1, 512], BF16, scopeH)
    wrow_use = [None]
    ps1 = p.psum(tag + "_ps1", [128, 512], scope=scopeH)
    ps2 = p.psum(tag + "_ps2", [128, 512], scope=scopeH)
    ps3 = p.psum(tag + "_ps3", [128, 512], scope=scopeH)

    wld_use = [None, None]
    ps1_use = None
    ps2_use = None
    ps3_use = None
    hid_use = [None, None]
    wbc_use = None
    par_i = 0
    for ei in range(NEXP + 1):
        shared = (ei == NEXP)
        Fe = FS if shared else F
        row0 = (NEXP * F) if shared else (ei * F)
        wg = shg_bf if shared else None
        wu = shu_bf if shared else None
        if not shared:
            wb_toks = []
            for sgi in range(NTS):
                tb1 = p.add("tensor", lambda e, ei=ei, sgi=sgi: e.matmul(
                    ps3[0:1, 0:512], p.sel8_bf[0:8, ei:ei + 1],
                    wmatT[0:8, sgi * 512:(sgi + 1) * 512],
                    start=True, stop=True),
                    fenceR + ([ps3_use] if ps3_use else []))
                tb2 = p.add("vector", lambda e: e.tensor_copy(
                    wrow[0:1, :], ps3[0:1, 0:512]), [tb1] +
                    ([wrow_use[0]] if wrow_use[0] else []), small=True)
                tb3 = p.add("tensor", lambda e: e.matmul(
                    ps3[:, :], p.ones_bf[0:1, :], wrow[0:1, :],
                    start=True, stop=True), [tb2])
                wrow_use[0] = tb3
                tc = p.add("vector", lambda e, sgi=sgi: e.tensor_copy(
                    wbc[:, sgi * 512:(sgi + 1) * 512], ps3[:, :]),
                    [tb3] + ([wbc_use] if wbc_use else []))
                ps3_use = tc
                wb_toks.append(tc)
        else:
            wb_toks = []
        nseg = ceil_div(Fe, 512)
        for fs in range(nseg):
            fw = min(512, Fe - fs * 512)
            par = par_i % 2
            par_i += 1
            l1 = []
            l2 = []
            for dc in range(16):
                dga = [wld_use[par]] if wld_use[par] else list(fenceR)
                if shared:
                    src1 = lambda e, fs=fs, dc=dc, fw=fw, par=par: e.dma_start(
                        out=w1t[par][dc][:, 0:fw], in_=shg_bf[fs, dc, :, 0:fw])
                    src2 = lambda e, fs=fs, dc=dc, fw=fw, par=par: e.dma_start(
                        out=w2t[par][dc][:, 0:fw], in_=shu_bf[fs, dc, :, 0:fw])
                else:
                    src1 = lambda e, ei=ei, fs=fs, dc=dc, fw=fw, par=par: e.dma_start(
                        out=w1t[par][dc][:, 0:fw], in_=gate_bf[ei, fs, dc, :, 0:fw])
                    src2 = lambda e, ei=ei, fs=fs, dc=dc, fw=fw, par=par: e.dma_start(
                        out=w2t[par][dc][:, 0:fw], in_=up_bf[ei, fs, dc, :, 0:fw])
                l1.append(p.dma(src1, dga, q="scalar", key=(tag + "_w1", par)))
                l2.append(p.dma(src2, dga, q="scalar", key=(tag + "_w2", par)))
            for fb in range(ceil_div(fw, 128)):
                fbw = min(128, fw - fb * 128)
                for ts in range(NTS):
                    hk = (fb * NTS + ts) % 2
                    m1 = p.add("tensor", lambda e, par=par, fb=fb, ts=ts, fbw=fbw: [
                        e.matmul(ps1[0:fbw, :],
                                 w1t[par][dc2][:, fb * 128:fb * 128 + fbw],
                                 xnf[dc2][:, ts * 512:(ts + 1) * 512],
                                 start=(dc2 == 0), stop=(dc2 == 15))
                        for dc2 in range(16)][-1],
                        l1 + xnf_toks + ([ps1_use] if ps1_use else []))
                    m2 = p.add("tensor", lambda e, par=par, fb=fb, ts=ts, fbw=fbw: [
                        e.matmul(ps2[0:fbw, :],
                                 w2t[par][dc2][:, fb * 128:fb * 128 + fbw],
                                 xnf[dc2][:, ts * 512:(ts + 1) * 512],
                                 start=(dc2 == 0), stop=(dc2 == 15))
                        for dc2 in range(16)][-1],
                        l2 + xnf_toks + ([ps2_use] if ps2_use else []))
                    tsg = p.add("scalar", lambda e, fbw=fbw: e.activation(
                        sgt[0:fbw, :], ps1[0:fbw, :], AF.Silu), [m1])
                    ps1_use = tsg
                    tmul = p.add("vector", lambda e, fbw=fbw: e.tensor_mul(
                        tmu[0:fbw, :], sgt[0:fbw, :], ps2[0:fbw, :]), [tsg, m2])
                    ps2_use = tmul
                    if shared:
                        th = p.add("vector", lambda e, hk=hk, fbw=fbw: e.tensor_copy(
                            hid[hk][0:fbw, :], tmu[0:fbw, :]),
                            [tmul] + ([hid_use[hk]] if hid_use[hk] else []))
                    else:
                        th = p.add("vector", lambda e, hk=hk, fbw=fbw, ts=ts: e.tensor_mul(
                            hid[hk][0:fbw, :], tmu[0:fbw, :],
                            wbc[0:fbw, ts * 512:(ts + 1) * 512]
                            if False else wbc[:, ts * 512:(ts + 1) * 512][0:fbw, :]),
                            [tmul] + wb_toks + ([hid_use[hk]] if hid_use[hk] else []))
                    tst = p.dma(lambda e, row0=row0, fs=fs, fb=fb, ts=ts, hk=hk,
                                fbw=fbw: e.dma_start(
                        out=hidden_dram[row0 + fs * 512 + fb * 128:
                                        row0 + fs * 512 + fb * 128 + fbw,
                                        ts * 512:(ts + 1) * 512], in_=hid[hk][0:fbw, :]),
                        [th], key=tag + "_hst")
                    hid_use[hk] = tst
            wld_use[par] = p.last("tensor")
        if not shared:
            wbc_use = p.last("vector")
    fenceH = p.fence()
    p.release(scopeH)
    p.release(scope0)

    scopeD = []
    NHC = 14080 // 128
    HHALF = 55
    hall = [p.sbuf(tag + "_hall", [128, 512], BF16, scopeD) for _ in range(HHALF)]
    w3t = [p.sbuf(tag + "_w3t", [128, 512], BF16, scopeD) for _ in range(HHALF)]
    accs = [p.sbuf(tag + "_acc", [128, D], F32, scopeD) for _ in range(4)]
    xs = [p.sbuf(tag + "_x", [128, D], F32, scopeD) for _ in range(4)]
    psd = [p.psum(tag + "_psd", [128, 512], scope=scopeD) for _ in range(4)]

    def down_tile_src(gl_fc):
        if gl_fc < NEXP * 11:
            ei = gl_fc // 11
            fc = gl_fc % 11
            return ("exp", ei, fc)
        return ("sh", 0, gl_fc - NEXP * 11)

    hall_use = [None] * HHALF
    w3_use = [None] * HHALF
    x_use = [None] * 4
    psd_use = [None] * 4
    acc_use = [None] * 4
    last = None
    for tg in range(NTS):
        xtoks = []
        for tt in range(4):
            dep = fenceH + ([x_use[tt]] if x_use[tt] else [])
            xtoks.append(p.dma(lambda e, tg=tg, tt=tt: e.dma_start(
                out=xs[tt][:], in_=x_dram[tg * 512 + tt * 128:tg * 512 + (tt + 1) * 128, :]),
                dep, key=tag + "_x"))
        for half in range(2):
            htoks = []
            for i in range(HHALF):
                gl = half * HHALF + i
                dep = fenceH + ([hall_use[i]] if hall_use[i] else [])
                htoks.append(p.dma(lambda e, gl=gl, i=i, tg=tg: e.dma_start(
                    out=hall[i][:, :],
                    in_=hidden_dram[gl * 128:(gl + 1) * 128, tg * 512:(tg + 1) * 512]),
                    dep, key=tag + "_hld"))
            for ds in range(4):
                w3toks = []
                for i in range(HHALF):
                    gl = half * HHALF + i
                    kind, ei, fc = down_tile_src(gl)
                    dep = fenceH + ([w3_use[i]] if w3_use[i] else [])
                    if kind == "exp":
                        fn = lambda e, ei=ei, fc=fc, i=i, ds=ds: e.dma_start(
                            out=w3t[i][:, :], in_=down_bf[ei, ds, fc, :, :])
                    else:
                        fn = lambda e, fc=fc, i=i, ds=ds: e.dma_start(
                            out=w3t[i][:, :], in_=shd_bf[ds, fc, :, :])
                    w3toks.append(p.dma(fn, dep, q="scalar",
                                        key=w3key or (tag + "_w3")))
                for tt in range(4):
                    bank = tt
                    mm = p.add("tensor", lambda e, tt=tt, bank=bank: [
                        e.matmul(psd[bank][:, :], hall[i2][:, tt * 128:(tt + 1) * 128],
                                 w3t[i2][:, :], start=(i2 == 0), stop=(i2 == HHALF - 1))
                        for i2 in range(HHALF)][-1],
                        htoks + w3toks + ([psd_use[bank]] if psd_use[bank] else []))
                    w3_use = [mm] * HHALF
                    hall_use = [mm] * HHALF
                    if half == 0:
                        tev = p.add("vector", lambda e, tt=tt, ds=ds, bank=bank:
                                    e.tensor_copy(accs[tt][:, ds * 512:(ds + 1) * 512],
                                                  psd[bank][:]),
                                    [mm] + ([acc_use[tt]] if acc_use[tt] else []))
                    else:
                        tev = p.add("vector", lambda e, tt=tt, ds=ds, bank=bank:
                                    e.tensor_add(accs[tt][:, ds * 512:(ds + 1) * 512],
                                                 psd[bank][:],
                                                 accs[tt][:, ds * 512:(ds + 1) * 512]),
                                    [mm] + ([acc_use[tt]] if acc_use[tt] else []))
                    psd_use[bank] = tev
                    acc_use[tt] = tev
        for tt in range(4):
            tadd = p.add("vector", lambda e, tt=tt: e.tensor_add(
                accs[tt][:, :], accs[tt][:, :], xs[tt][:, :]),
                [acc_use[tt], xtoks[tt]])
            ts2 = p.dma(lambda e, tg=tg, tt=tt: e.dma_start(
                out=xout_dram[tg * 512 + tt * 128:tg * 512 + (tt + 1) * 128, :],
                in_=accs[tt][:]), [tadd], key=tag + "_st")
            acc_use[tt] = ts2
            x_use[tt] = ts2
            last = ts2
    fenceD = p.fence()
    p.release(scopeD)
    return last, fenceD


def build_moe_test_launch():
    p = Prog()
    F, FS = 1408, 2816
    xn = p.dram("xn", [T, D], BF16, kind="ExternalInput")
    gwt = p.dram("gwt", [D, 8], F32, kind="ExternalInput")
    gate_w = p.dram("gate_w", [8, F, D], F32, kind="ExternalInput")
    up_w = p.dram("up_w", [8, F, D], F32, kind="ExternalInput")
    down_w = p.dram("down_w", [8, D, F], F32, kind="ExternalInput")
    shg = p.dram("shg", [FS, D], F32, kind="ExternalInput")
    shu = p.dram("shu", [FS, D], F32, kind="ExternalInput")
    shd = p.dram("shd", [D, FS], F32, kind="ExternalInput")
    x = p.dram("x", [T, D], F32, kind="ExternalInput")
    xout = p.dram("xout", [T, D], F32, kind="ExternalOutput")
    gwt_bf = p.dram("gwt_bf", [D, 8], BF16)
    gate_bf = p.dram("gate_bf", [8, F, D], BF16)
    up_bf = p.dram("up_bf", [8, F, D], BF16)
    down_bf = p.dram("down_bf", [8, D, F], BF16)
    shg_bf = p.dram("shg_bf", [FS, D], BF16)
    shu_bf = p.dram("shu_bf", [FS, D], BF16)
    shd_bf = p.dram("shd_bf", [D, FS], BF16)
    hidden = p.dram("hidden", [14080, T], BF16, kind="ExternalOutput")
    wmT_out = p.dram("wmT_out", [8, T], BF16, kind="ExternalOutput")
    toks = []
    toks += emit_cast(p, gwt, gwt_bf, D, 8, tag="c0")
    toks += emit_cast(p, gate_w.rearrange("a b c -> (a b) c"),
                      gate_bf.rearrange("a b c -> (a b) c"), 8 * F, D, tag="c1")
    toks += emit_cast(p, up_w.rearrange("a b c -> (a b) c"),
                      up_bf.rearrange("a b c -> (a b) c"), 8 * F, D, tag="c2")
    toks += emit_cast(p, down_w.rearrange("a b c -> (a b) c"),
                      down_bf.rearrange("a b c -> (a b) c"), 8 * D, F, tag="c3")
    toks += emit_cast(p, shg, shg_bf, FS, D, tag="c4")
    toks += emit_cast(p, shu, shu_bf, FS, D, tag="c5")
    toks += emit_cast(p, shd, shd_bf, D, FS, tag="c6")
    ext = {}
    dbg2 = p.dram("dbg2", [128, 48], F32, kind="ExternalOutput")
    emit_moe(p, xn, gwt_bf, gate_bf, up_bf, down_bf, shg_bf, shu_bf, shd_bf,
             hidden, x, xout, T, deps=toks, dbg=wmT_out, dbg2=dbg2)
    import os
    return p.build(debug_serialize=os.environ.get("SER", "0") == "1")


def emit_linattn(p, qf_dram, kf_dram, v_dram, av_dram, n_heads, n_tok,
                 mask_dram, deps=(), tag="la"):
    NC = n_tok // 128
    scope = []
    qf = p.sbuf(tag + "_qf", [128, n_tok], BF16, scope)
    kf = p.sbuf(tag + "_kf", [128, n_tok], BF16, scope)
    kftm = [p.sbuf(tag + "_kt", [128, 128], BF16, scope) for _ in range(2)]
    vt = [p.sbuf(tag + "_vt", [128, 129], BF16, scope) for _ in range(2)]
    at_sb = p.sbuf(tag + "_at", [128, 128], BF16, scope)
    mask = p.sbuf(tag + "_mk", [128, 128], BF16, scope)
    st_f = p.sbuf(tag + "_stf", [128, 129], F32, scope)
    st_b = p.sbuf(tag + "_stb", [128, 129], BF16, scope)
    zcol = p.sbuf(tag + "_z", [128, 1], F32, scope)
    rcol = p.sbuf(tag + "_r", [128, 1], F32, scope)
    avt = [p.sbuf(tag + "_av", [128, 128], BF16, scope) for _ in range(2)]
    ps_a = p.psum(tag + "_psa", [128, 128], scope=scope)
    ps_o = p.psum(tag + "_pso", [128, 129], scope=scope)
    ps_kv = p.psum(tag + "_psk", [128, 129], scope=scope)
    p.want_const(1e-6)

    tmk = p.dma(lambda e: e.dma_start(out=mask[:, :], in_=mask_dram[:, :]),
                deps, q="gpsimd", key=tag + "_mk")
    ones_toks = [p.add("vector", lambda e, i=i: e.memset(vt[i][:, 128:129], 1.0),
                       deps, small=True) for i in range(2)]
    last = None
    qkv_use = []
    for h in range(n_heads):
        dep_h = list(deps) + qkv_use
        tq = p.dma(lambda e, h=h: e.dma_start(
            out=qf[:, 0:n_tok], in_=qf_dram[0:n_tok, h * 128:(h + 1) * 128],
            transpose=True), dep_h, q="scalar", key=tag + "_q")
        tk = p.dma(lambda e, h=h: e.dma_start(
            out=kf[:, 0:n_tok], in_=kf_dram[0:n_tok, h * 128:(h + 1) * 128],
            transpose=True), dep_h, q="scalar", key=tag + "_k")
        tz1 = p.add("vector", lambda e: e.memset(st_f[:, :], 0.0), dep_h, small=True)
        tz2 = p.add("vector", lambda e: e.memset(st_b[:, :], 0.0), dep_h, small=True)
        st_b_use = tz2
        st_f_use = tz1
        kt_use = [None, None]
        vt_use = [None, None]
        at_use = None
        pso_use = None
        psa_use = None
        pskv_use = None
        av_use = [None, None]
        for c in range(NC):
            bk = c % 2
            tkt = p.dma(lambda e, h=h, c=c, bk=bk: e.dma_start(
                out=kftm[bk][:, :], in_=kf_dram[c * 128:(c + 1) * 128,
                                               h * 128:(h + 1) * 128]),
                dep_h + ([kt_use[bk]] if kt_use[bk] else []), q="sync",
                key=(tag + "_kt", bk))
            tvt = p.dma(lambda e, h=h, c=c, bk=bk: e.dma_start(
                out=vt[bk][:, 0:128], in_=v_dram[c * 128:(c + 1) * 128,
                                                 h * 128:(h + 1) * 128]),
                dep_h + [ones_toks[bk]] + ([vt_use[bk]] if vt_use[bk] else []),
                q="sync", key=(tag + "_vt", bk))
            ta = p.add("tensor", lambda e, c=c: e.matmul(
                ps_a[:, :], kf[:, c * 128:(c + 1) * 128],
                qf[:, c * 128:(c + 1) * 128], start=True, stop=True),
                [tq, tk] + ([psa_use] if psa_use else []))
            tam = p.add("vector", lambda e: e.tensor_mul(
                at_sb[:, :], ps_a[:, :], mask[:, :]),
                [ta, tmk] + ([at_use] if at_use else []))
            psa_use = tam
            to1 = p.add("tensor", lambda e, c=c: e.matmul(
                ps_o[:, 0:129], qf[:, c * 128:(c + 1) * 128], st_b[:, :],
                start=True, stop=False), [tq, st_b_use] +
                ([pso_use] if pso_use else []))
            to2 = p.add("tensor", lambda e, bk=bk: e.matmul(
                ps_o[:, 0:129], at_sb[:, :], vt[bk][:, :],
                start=False, stop=True), [tam, tvt, to1])
            at_use = to2
            tz = p.add("vector", lambda e: e.tensor_copy(zcol[:, 0:1],
                                                         ps_o[:, 128:129]),
                       [to2], small=True)
            tze = p.add("vector", lambda e: e.tensor_scalar_add(
                zcol[:, 0:1], zcol[:, 0:1], 1e-6), [tz], small=True)
            trc = p.add("vector", lambda e: e.reciprocal(rcol[:, 0:1], zcol[:, 0:1]),
                        [tze], small=True)
            tao = p.add("scalar", lambda e, bk=bk: e.activation(
                avt[bk][:, :], ps_o[:, 0:128], AF.Copy, scale=rcol[:, 0:1]),
                [trc] + ([av_use[bk]] if av_use[bk] else []))
            pso_use = tao
            tsa = p.dma(lambda e, h=h, c=c, bk=bk: e.dma_start(
                out=av_dram[c * 128:(c + 1) * 128, h * 128:(h + 1) * 128],
                in_=avt[bk][:, :]), [tao], key=tag + "_avst")
            av_use[bk] = tsa
            last = tsa
            tkv = p.add("tensor", lambda e, bk=bk: e.matmul(
                ps_kv[:, 0:129], kftm[bk][:, :], vt[bk][:, :],
                start=True, stop=True),
                [tkt, tvt] + ([pskv_use] if pskv_use else []))
            kt_use[bk] = tkv
            vt_use[bk] = tkv if False else None
            vt_use[bk] = tkv
            tsu = p.add("vector", lambda e: e.tensor_add(
                st_f[:, :], ps_kv[:, 0:129], st_f[:, :]), [tkv, st_f_use])
            pskv_use = tsu
            tsb = p.add("vector", lambda e: e.tensor_copy(st_b[:, :], st_f[:, :]),
                        [tsu, to1])
            st_b_use = tsb
            st_f_use = tsu
        qkv_use = [p.last("tensor"), p.last("vector"), p.last("scalar")]
    fence = p.fence()
    p.release(scope)
    return last, fence


def emit_winattn(p, q_dram, k_dram, v_dram, av_dram, n_heads, n_kv, n_tok,
                 lam_dram, wmask_dram, deps=(), tag="wa"):
    NQT = n_tok // 128
    WIN = 640
    scale = 1.0 / 8.0
    rep = n_heads // n_kv
    scope = []
    qf = p.sbuf(tag + "_qf", [128, n_tok], BF16, scope)
    kf = p.sbuf(tag + "_kf", [128, n_tok], BF16, scope)
    vt = [p.sbuf(tag + "_vt", [128, 129], BF16, scope) for _ in range(n_tok // 128)]
    wmask = [p.sbuf(tag + "_wm", [128, WIN], F32, scope) for _ in range(5)]
    lam_bc = [p.sbuf(tag + "_lb", [128, 1], F32, scope) for _ in range(n_heads)]
    p1 = p.sbuf(tag + "_p1", [128, WIN], BF16, scope)
    p2 = p.sbuf(tag + "_p2", [128, WIN], BF16, scope)
    t1 = p.sbuf(tag + "_t1", [128, WIN], F32, scope)
    t2 = p.sbuf(tag + "_t2", [128, WIN], F32, scope)
    df = p.sbuf(tag + "_df", [128, WIN], BF16, scope)
    dT = [p.sbuf(tag + "_dT", [128, 128], BF16, scope) for _ in range(5)]
    l1 = p.sbuf(tag + "_l1", [128, 1], F32, scope)
    l2 = p.sbuf(tag + "_l2", [128, 1], F32, scope)
    r1 = p.sbuf(tag + "_r1", [128, 1], F32, scope)
    rl2n = p.sbuf(tag + "_rl", [128, 1], F32, scope)
    zc = p.sbuf(tag + "_z", [128, 1], F32, scope)
    rc = p.sbuf(tag + "_rc", [128, 1], F32, scope)
    avt = [p.sbuf(tag + "_av", [128, 128], BF16, scope) for _ in range(2)]
    ps_s1 = p.psum(tag + "_s1", [128, WIN], scope=scope)
    ps_s2 = p.psum(tag + "_s2", [128, WIN], scope=scope)
    ps_t = p.psum(tag + "_pt", [128, 128], BF16, scope=scope)
    ps_o = p.psum(tag + "_po", [128, 129], scope=scope)
    p.want_const(1e-6, -SOFTMAX_SHIFT)

    wm_toks = []
    for i in range(5):
        wm_toks.append(p.dma(lambda e, i=i: e.dma_start(
            out=wmask[i][:, :], in_=wmask_dram[i, :, :]), deps, q="sync",
            key=tag + "_wm"))
    lam_toks = []
    for h in range(n_heads):
        bc_ap = bass.AP(tensor=lam_dram.tensor, offset=h, ap=[[0, 128], [1, 1]])
        lam_toks.append(p.dma(lambda e, h=h, bc_ap=bc_ap: e.dma_start(
            out=lam_bc[h][:, 0:1], in_=bc_ap, allow_slow_non_contiguous=True),
            deps, q="gpsimd", key=tag + "_lam"))
    ones_toks = [p.add("vector", lambda e, kc=kc: e.memset(vt[kc][:, 128:129], 1.0),
                       deps, small=True) for kc in range(n_tok // 128)]

    last = None
    qkv_use = []
    for h in range(n_heads):
        kvh = h // rep
        dep_h = list(deps) + qkv_use
        tq1 = p.dma(lambda e, h=h: e.dma_start(
            out=qf[:, 0:n_tok], in_=q_dram[0:n_tok, h * 128:(h + 1) * 128],
            transpose=True), dep_h, q="scalar", key=tag + "_q")
        tq2 = tq1
        tk1 = p.dma(lambda e, kvh=kvh: e.dma_start(
            out=kf[:, 0:n_tok], in_=k_dram[0:n_tok, kvh * 128:(kvh + 1) * 128],
            transpose=True), dep_h, q="scalar", key=tag + "_k")
        tk2 = tk1
        vtoks = []
        for kc in range(n_tok // 128):
            vtoks.append(p.dma(lambda e, kvh=kvh, kc=kc: e.dma_start(
                out=vt[kc][:, 0:128],
                in_=v_dram[kc * 128:(kc + 1) * 128, kvh * 128:(kvh + 1) * 128]),
                dep_h + [ones_toks[kc]], q="sync", key=tag + "_v"))
        qkv_use = []
        s1_use = None
        s2_use = None
        p1_use = None
        p2_use = None
        d_use = None
        dT_use = [None] * 5
        pst_use = None
        pso_use = None
        av_use = [None, None]
        for qt in range(NQT):
            w0 = max(0, (qt - 4)) * 128 if qt >= 4 else 0
            mi = min(qt, 4)
            def mk_s(e, qt=qt, w0=w0, half=0, ps_=None):
                d0 = half * 64
                e.matmul(ps_[:, 0:512], qf[d0:d0 + 64, qt * 128:(qt + 1) * 128],
                         kf[d0:d0 + 64, w0:w0 + 512], start=True, stop=True)
                return e.matmul(ps_[:, 512:WIN],
                                qf[d0:d0 + 64, qt * 128:(qt + 1) * 128],
                                kf[d0:d0 + 64, w0 + 512:w0 + WIN],
                                start=True, stop=True)
            ts1 = p.add("tensor", lambda e, qt=qt, w0=w0: mk_s(
                e, qt, w0, half=0, ps_=ps_s1),
                [tq1, tk1] + ([s1_use] if s1_use else []))
            ts2 = p.add("tensor", lambda e, qt=qt, w0=w0: mk_s(
                e, qt, w0, half=1, ps_=ps_s2),
                [tq2, tk2] + ([s2_use] if s2_use else []))
            tm1 = p.add("vector", lambda e, mi=mi: e.tensor_add(
                ps_s1[:, 0:WIN], ps_s1[:, 0:WIN], wmask[mi][:, :]),
                [ts1, wm_toks[-1]])
            tm2 = p.add("vector", lambda e, mi=mi: e.tensor_add(
                ps_s2[:, 0:WIN], ps_s2[:, 0:WIN], wmask[mi][:, :]),
                [ts2, wm_toks[-1]])
            te1 = p.add("scalar", lambda e: e.activation(
                p1[:, :], ps_s1[:, 0:WIN], AF.Exp, bias=p.c(-SOFTMAX_SHIFT),
                scale=scale, accum_out=l1[:, 0:1]),
                [tm1] + ([p1_use] if p1_use else []))
            s1_use = te1
            te2 = p.add("scalar", lambda e: e.activation(
                p2[:, :], ps_s2[:, 0:WIN], AF.Exp, bias=p.c(-SOFTMAX_SHIFT),
                scale=scale, accum_out=l2[:, 0:1]),
                [tm2] + ([p2_use] if p2_use else []))
            s2_use = te2
            tr1 = p.add("vector", lambda e: e.reciprocal(r1[:, 0:1], l1[:, 0:1]),
                        [te1], small=True)
            tr2 = p.add("vector", lambda e: e.reciprocal(rl2n[:, 0:1], l2[:, 0:1]),
                        [te2], small=True)
            tr3 = p.add("vector", lambda e, h=h: e.tensor_scalar(
                rl2n[:, 0:1], rl2n[:, 0:1], lam_bc[h][:, 0:1], -1.0,
                ALU.mult, ALU.mult), [tr2, lam_toks[h]], small=True)
            tt1 = p.add("scalar", lambda e: e.activation(
                t1[:, :], p1[:, :], AF.Copy, scale=r1[:, 0:1]), [tr1, te1])
            p1_use = tt1
            tt2 = p.add("scalar", lambda e: e.activation(
                t2[:, :], p2[:, :], AF.Copy, scale=rl2n[:, 0:1]), [tr3, te2])
            p2_use = tt2
            td = p.add("vector", lambda e: e.tensor_add(t1[:, :], t1[:, :], t2[:, :]),
                       [tt1, tt2])
            tdr = p.add("vector", lambda e: e.tensor_scalar_max(
                df[:, :], t1[:, :], 0.0), [td] + ([d_use] if d_use else []))
            avdeps = []
            for kc in range(5):
                ttr = p.add("tensor", lambda e, kc=kc: e.transpose(
                    ps_t[:, :], df[:, kc * 128:(kc + 1) * 128], p.ident_bf[:, :]),
                    [tdr] + ([pst_use] if pst_use else []))
                tev = p.add("vector", lambda e, kc=kc: e.tensor_copy(
                    dT[kc][:, :], ps_t[:, :]), [ttr] +
                    ([dT_use[kc]] if dT_use[kc] else []))
                pst_use = tev
                dT_use[kc] = tev
                avdeps.append(tev)
            d_use = avdeps[-1]
            kc0 = w0 // 128
            mms = []
            for i2 in range(5):
                kg = kc0 + i2
                mms.append(p.add("tensor", lambda e, i2=i2, kg=kg: e.matmul(
                    ps_o[:, 0:129], dT[i2][:, :], vt[kg][:, :],
                    start=(i2 == 0), stop=(i2 == 4)),
                    [avdeps[i2], vtoks[kg]] + ([pso_use] if pso_use and i2 == 0 else [])))
            dT_use = [mms[-1]] * 5
            tz = p.add("vector", lambda e: e.tensor_copy(zc[:, 0:1], ps_o[:, 128:129]),
                       [mms[-1]], small=True)
            tze = p.add("vector", lambda e: e.tensor_scalar_add(
                zc[:, 0:1], zc[:, 0:1], 1e-6), [tz], small=True)
            trc = p.add("vector", lambda e: e.reciprocal(rc[:, 0:1], zc[:, 0:1]),
                        [tze], small=True)
            bk = qt % 2
            tao = p.add("scalar", lambda e, bk=bk: e.activation(
                avt[bk][:, :], ps_o[:, 0:128], AF.Copy, scale=rc[:, 0:1]),
                [trc] + ([av_use[bk]] if av_use[bk] else []))
            pso_use = tao
            tsa = p.dma(lambda e, h=h, qt=qt, bk=bk: e.dma_start(
                out=av_dram[qt * 128:(qt + 1) * 128, h * 128:(h + 1) * 128],
                in_=avt[bk][:, :]), [tao], key=tag + "_avst")
            av_use[bk] = tsa
            last = tsa
        qkv_use = [p.last("tensor"), p.last("vector"), p.last("scalar")]
    fence = p.fence()
    p.release(scope)
    return last, fence


def emit_diffattn(p, q_dram, k_dram, v_dram, av_dram, n_heads, n_kv, n_tok,
                  lq1_dram, lk1_dram, lq2_dram, lk2_dram, gnw_dram, gnb_dram,
                  mask_dram, scr_dram, deps=(), tag="da"):
    NQB = n_tok // 256
    rep = n_heads // n_kv
    LAMBDA_INIT = 0.8 - 0.6 * float(np.exp(-0.3 * 0))
    scale = 1.0 / np.sqrt(128.0)
    NTILE = n_tok // 128
    scope = []
    qf1 = p.sbuf(tag + "_q1", [128, n_tok], BF16, scope)
    qf2 = p.sbuf(tag + "_q2", [128, n_tok], BF16, scope)
    kf1 = p.sbuf(tag + "_k1", [128, n_tok], BF16, scope)
    kf2 = p.sbuf(tag + "_k2", [128, n_tok], BF16, scope)
    vt = [p.sbuf(tag + "_vt", [128, 257], BF16, scope) for _ in range(NTILE)]
    pt1 = [p.sbuf(tag + "_p1", [128, 256], BF16, scope) for _ in range(4)]
    pt2 = [p.sbuf(tag + "_p2", [128, 256], BF16, scope) for _ in range(4)]
    mask = p.sbuf(tag + "_mk", [128, 128], BF16, scope)
    nshift = p.sbuf(tag + "_ns", [128, 1], F32, scope)
    lrow = [p.sbuf(tag + "_lr", [1, 128], F32, scope) for _ in range(2)]
    lsc = p.sbuf(tag + "_lsc", [1, 8], F32, scope)
    lam_bc = p.sbuf(tag + "_lb", [128, 1], F32, scope)
    r1c = p.sbuf(tag + "_r1", [128, 1], F32, scope)
    rl2 = p.sbuf(tag + "_rl", [128, 1], F32, scope)
    tt1 = p.sbuf(tag + "_t1", [128, 256], F32, scope)
    tt2 = p.sbuf(tag + "_t2", [128, 256], F32, scope)
    sqs = p.sbuf(tag + "_sq", [128, 256], F32, scope)
    avh = p.sbuf(tag + "_avh", [128, 256 * NTILE], BF16, scope)
    sums = p.sbuf(tag + "_sm", [128, NTILE], F32, scope)
    sqsum = p.sbuf(tag + "_sq2", [128, NTILE], F32, scope)
    hrow = p.sbuf(tag + "_hr", [1, 128], F32, scope)
    hsc = p.sbuf(tag + "_hsc", [1, 8], F32, scope)
    grow = p.sbuf(tag + "_gr", [1, 256], F32, scope)
    brow = p.sbuf(tag + "_br", [1, 256], F32, scope)
    srow_b = p.sbuf(tag + "_srb", [1, 256], BF16, scope)
    brow_b = p.sbuf(tag + "_brb", [1, 256], BF16, scope)
    sbc = p.sbuf(tag + "_sbc", [128, 256], BF16, scope)
    bbc = p.sbuf(tag + "_bbc", [128, 256], BF16, scope)
    avo = p.sbuf(tag + "_avo", [128, 256], BF16, scope)
    st1 = p.psum(tag + "_s1", [128, 256], scope=scope)
    st2 = p.psum(tag + "_s2", [128, 256], scope=scope)
    acc1 = [p.psum(tag + "_a1", [128, 257], scope=scope) for _ in range(2)]
    acc2 = [p.psum(tag + "_a2", [128, 257], scope=scope) for _ in range(2)]
    ps_bc = p.psum(tag + "_pbc", [128, 256], scope=scope)
    p.want_const(-SOFTMAX_SHIFT, 1e-5)

    scrA = p.dram(p.name("da_scrA"), [128, 1], F32)
    scrB = p.dram(p.name("da_scrB"), [128, 1], F32)
    tmk = p.dma(lambda e: e.dma_start(out=mask[:, :], in_=mask_dram[:, :]),
                deps, q="gpsimd", key=tag + "_mk")
    tns = p.add("vector", lambda e: e.memset(nshift[:, 0:1], -SOFTMAX_SHIFT),
                deps, small=True)
    ones_toks = [p.add("vector", lambda e, kc=kc: e.memset(vt[kc][:, 256:257], 1.0),
                       deps, small=True) for kc in range(NTILE)]

    lam_toks = []
    l_tmp = [p.sbuf(tag + "_ltmp", [1, 128], F32, scope) for _ in range(2)]
    for i2, (aa, bb) in enumerate(((lq1_dram, lk1_dram), (lq2_dram, lk2_dram))):
        ta = p.dma(lambda e, aa=aa, i2=i2: e.dma_start(
            out=lrow[i2][0:1, :], in_=aa[None, :]), deps, q="gpsimd",
            key=tag + "_lv")
        tb = p.dma(lambda e, bb=bb, i2=i2: e.dma_start(
            out=l_tmp[i2][0:1, :], in_=bb[None, :]), deps, q="gpsimd",
            key=tag + "_lv")
        tm = p.add("vector", lambda e, i2=i2: e.tensor_mul(
            lrow[i2][0:1, :], lrow[i2][0:1, :], l_tmp[i2][0:1, :]),
            [ta, tb], small=True)
        tr = p.add("vector", lambda e, i2=i2: e.reduce_sum(
            lsc[0:1, i2:i2 + 1], lrow[i2][0:1, :], axis=AX.X), [tm], small=True)
        te = p.add("scalar", lambda e, i2=i2: e.activation(
            lsc[0:1, 2 + i2:3 + i2], lsc[0:1, i2:i2 + 1], AF.Exp), [tr], small=True)
        lam_toks.append(te)
    tl1 = p.add("vector", lambda e: e.tensor_sub(
        lsc[0:1, 4:5], lsc[0:1, 2:3], lsc[0:1, 3:4]), lam_toks, small=True)
    tl2 = p.add("vector", lambda e: e.tensor_scalar_add(
        lsc[0:1, 4:5], lsc[0:1, 4:5], LAMBDA_INIT), [tl1], small=True)
    tl3 = p.dma(lambda e: e.dma_start(out=scr_dram[0:1, 0:8], in_=lsc[0:1, 0:8]),
                [tl2], key=tag + "_scr")
    bc_ap = bass.AP(tensor=scr_dram.tensor, offset=4, ap=[[0, 128], [1, 1]])
    tl4 = p.dma(lambda e: e.dma_start(out=lam_bc[:, 0:1], in_=bc_ap,
                                      allow_slow_non_contiguous=True),
                [tl3], q="gpsimd", key=tag + "_scr")

    last = None
    qkv_use = []
    for h in range(n_heads):
        kvh = h // rep
        dep_h = list(deps) + qkv_use
        tq1 = p.dma(lambda e, h=h: e.dma_start(
            out=qf1[:, 0:n_tok], in_=q_dram[0:n_tok, h * 256:h * 256 + 128],
            transpose=True), dep_h, q="scalar", key=tag + "_qk")
        tq2 = p.dma(lambda e, h=h: e.dma_start(
            out=qf2[:, 0:n_tok], in_=q_dram[0:n_tok, h * 256 + 128:h * 256 + 256],
            transpose=True), dep_h, q="scalar", key=tag + "_qk")
        tk1 = p.dma(lambda e, kvh=kvh: e.dma_start(
            out=kf1[:, 0:n_tok], in_=k_dram[0:n_tok, kvh * 256:kvh * 256 + 128],
            transpose=True), dep_h, q="scalar", key=tag + "_qk")
        tk2 = p.dma(lambda e, kvh=kvh: e.dma_start(
            out=kf2[:, 0:n_tok], in_=k_dram[0:n_tok, kvh * 256 + 128:kvh * 256 + 256],
            transpose=True), dep_h, q="scalar", key=tag + "_qk")
        vtoks = []
        for kc in range(NTILE):
            vtoks.append(p.dma(lambda e, kvh=kvh, kc=kc: e.dma_start(
                out=vt[kc][:, 0:256],
                in_=v_dram[kc * 128:(kc + 1) * 128, kvh * 256:(kvh + 1) * 256]),
                dep_h + [ones_toks[kc]], q="sync", key=tag + "_v"))
        qkv_use = []
        st1_use = None
        st2_use = None
        pt1_use = [None] * 4
        pt2_use = [None] * 4
        acc_mm1 = [None, None]
        acc_mm2 = [None, None]
        acc_free1 = [None, None]
        acc_free2 = [None, None]
        tile_done = []
        for qb in range(NQB):
            qtg = [qb * 2, qb * 2 + 1]
            first1 = [True, True]
            first2 = [True, True]
            nkb = (qb * 2 + 1) // 4 + 1
            for kb in range(nkb):
                for kc in range(4):
                    kg = kb * 4 + kc
                    if kg > qtg[1]:
                        continue
                    tsa = p.add("tensor", lambda e, kg=kg, qb=qb: e.matmul(
                        st1[:, 0:256], kf1[:, kg * 128:(kg + 1) * 128],
                        qf1[:, qb * 256:(qb + 1) * 256], start=True, stop=True),
                        [tq1, tk1] + ([st1_use] if st1_use else []))
                    te1 = p.add("scalar", lambda e, kc=kc: e.activation(
                        pt1[kc][:, :], st1[:, 0:256], AF.Exp, bias=nshift[:, 0:1],
                        scale=scale), [tsa, tns] +
                        ([pt1_use[kc]] if pt1_use[kc] else []))
                    st1_use = te1
                    tsb = p.add("tensor", lambda e, kg=kg, qb=qb: e.matmul(
                        st2[:, 0:256], kf2[:, kg * 128:(kg + 1) * 128],
                        qf2[:, qb * 256:(qb + 1) * 256], start=True, stop=True),
                        [tq2, tk2] + ([st2_use] if st2_use else []))
                    te2 = p.add("scalar", lambda e, kc=kc: e.activation(
                        pt2[kc][:, :], st2[:, 0:256], AF.Exp, bias=nshift[:, 0:1],
                        scale=scale), [tsb, tns] +
                        ([pt2_use[kc]] if pt2_use[kc] else []))
                    st2_use = te2
                    for qt in range(2):
                        if kg == qtg[qt]:
                            te1 = p.add("vector", lambda e, kc=kc, qt=qt: e.tensor_mul(
                                pt1[kc][:, qt * 128:(qt + 1) * 128],
                                pt1[kc][:, qt * 128:(qt + 1) * 128], mask[:, :]),
                                [te1, tmk])
                            te2 = p.add("vector", lambda e, kc=kc, qt=qt: e.tensor_mul(
                                pt2[kc][:, qt * 128:(qt + 1) * 128],
                                pt2[kc][:, qt * 128:(qt + 1) * 128], mask[:, :]),
                                [te2, tmk])
                    pt1_use[kc] = te1
                    pt2_use[kc] = te2
                for qt in range(2):
                    for kc in range(4):
                        kg = kb * 4 + kc
                        if kg > qtg[qt]:
                            continue
                        lastk = (kg == qtg[qt]) or (kb == nkb - 1 and (
                            kb * 4 + 3 >= qtg[qt] or kc == 3) and kg == min(qtg[qt], kb * 4 + 3))
                        isl = (kg == min(qtg[qt], nkb * 4 - 1))
                        m1 = p.add("tensor", lambda e, kg=kg, kc=kc, qt=qt,
                                   f=first1[qt], l=isl: e.matmul(
                            acc1[qt][:, 0:257], pt1[kc][:, qt * 128:(qt + 1) * 128],
                            vt[kg][:, :], start=f, stop=l),
                            [pt1_use[kc], vtoks[kg], acc_mm1[qt]] +
                            ([acc_free1[qt]] if first1[qt] else []))
                        first1[qt] = False
                        acc_mm1[qt] = m1
                        pt1_use[kc] = m1
                        m2 = p.add("tensor", lambda e, kg=kg, kc=kc, qt=qt,
                                   f=first2[qt], l=isl: e.matmul(
                            acc2[qt][:, 0:257], pt2[kc][:, qt * 128:(qt + 1) * 128],
                            vt[kg][:, :], start=f, stop=l),
                            [pt2_use[kc], vtoks[kg], acc_mm2[qt]] +
                            ([acc_free2[qt]] if first2[qt] else []))
                        first2[qt] = False
                        acc_mm2[qt] = m2
                        pt2_use[kc] = m2
            for qt in range(2):
                gq = qb * 2 + qt
                trc1 = p.add("vector", lambda e, qt=qt: e.reciprocal(
                    r1c[:, 0:1], acc1[qt][:, 256:257]), [acc_mm1[qt]], small=True)
                trc2 = p.add("vector", lambda e, qt=qt: e.reciprocal(
                    rl2[:, 0:1], acc2[qt][:, 256:257]), [acc_mm2[qt]], small=True)
                trc3 = p.add("vector", lambda e: e.tensor_scalar(
                    rl2[:, 0:1], rl2[:, 0:1], lam_bc[:, 0:1], -1.0,
                    ALU.mult, ALU.mult), [trc2, tl4], small=True)
                tc1 = p.add("scalar", lambda e, qt=qt: e.activation(
                    tt1[:, :], acc1[qt][:, 0:256], AF.Copy, scale=r1c[:, 0:1]),
                    [trc1])
                tc2 = p.add("scalar", lambda e, qt=qt: e.activation(
                    tt2[:, :], acc2[qt][:, 0:256], AF.Copy, scale=rl2[:, 0:1]),
                    [trc3])
                acc_free1[qt] = tc1
                acc_free2[qt] = tc2
                tpre = p.add("vector", lambda e: e.tensor_add(
                    tt1[:, :], tt1[:, :], tt2[:, :]), [tc1, tc2])
                tsum = p.add("vector", lambda e, gq=gq: e.reduce_sum(
                    sums[:, gq:gq + 1], tt1[:, :], axis=AX.X), [tpre], small=True)
                tsq = p.add("scalar", lambda e, gq=gq: e.activation(
                    sqs[:, :], tt1[:, :], AF.Square, accum_out=sqsum[:, gq:gq + 1]),
                    [tpre], small=True)
                tst = p.add("vector", lambda e, gq=gq: e.tensor_copy(
                    avh[:, gq * 256:(gq + 1) * 256], tt1[:, :]), [tpre])
                tile_done += [tsum, tsq, tst]
        hs1 = p.add("vector", lambda e: e.reduce_sum(r1c[:, 0:1], sums[:, :],
                                                     axis=AX.X),
                    tile_done, small=True)
        hs2 = p.add("vector", lambda e: e.reduce_sum(rl2[:, 0:1], sqsum[:, :],
                                                     axis=AX.X),
                    tile_done, small=True)
        hd1 = p.dma(lambda e: e.dma_start(out=scrA[:, 0:1], in_=r1c[:, 0:1]),
                    [hs1], key=tag + "_scr")
        hd2 = p.dma(lambda e: e.dma_start(out=scrB[:, 0:1], in_=rl2[:, 0:1]),
                    [hs2], key=tag + "_scr")
        hr1 = p.dma(lambda e: e.dma_start(
            out=hrow[0:1, :], in_=scrA.rearrange("a b -> b a")), [hd1],
            key=tag + "_scr")
        hv1 = p.add("vector", lambda e: e.reduce_sum(hsc[0:1, 0:1], hrow[0:1, :],
                                                     axis=AX.X), [hr1], small=True)
        hr2 = p.dma(lambda e: e.dma_start(
            out=hrow[0:1, :], in_=scrB.rearrange("a b -> b a")), [hd2, hv1],
            key=tag + "_scr")
        hv2 = p.add("vector", lambda e: e.reduce_sum(hsc[0:1, 1:2], hrow[0:1, :],
                                                     axis=AX.X), [hr2], small=True)
        NTOT = float(n_tok * 256)
        hm = p.add("vector", lambda e: e.tensor_scalar_mul(
            hsc[0:1, 2:3], hsc[0:1, 0:1], 1.0 / NTOT), [hv1], small=True)
        hmq = p.add("vector", lambda e: e.tensor_mul(
            hsc[0:1, 3:4], hsc[0:1, 2:3], hsc[0:1, 2:3]), [hm], small=True)
        hvv = p.add("vector", lambda e: e.tensor_scalar_mul(
            hsc[0:1, 4:5], hsc[0:1, 1:2], 1.0 / NTOT), [hv2], small=True)
        hvar = p.add("vector", lambda e: e.tensor_sub(
            hsc[0:1, 4:5], hsc[0:1, 4:5], hsc[0:1, 3:4]), [hvv, hmq], small=True)
        hln = p.add("scalar", lambda e: e.activation(
            hsc[0:1, 5:6], hsc[0:1, 4:5], AF.Ln, bias=p.c(1e-5)[0:1, 0:1]),
            [hvar], small=True)
        hrs = p.add("scalar", lambda e: e.activation(
            hsc[0:1, 6:7], hsc[0:1, 5:6], AF.Exp, scale=-0.5), [hln], small=True)
        hms = p.add("vector", lambda e: e.tensor_mul(
            hsc[0:1, 7:8], hsc[0:1, 2:3], hsc[0:1, 6:7]), [hrs], small=True)
        tgw = p.dma(lambda e, h=h: e.dma_start(
            out=grow[0:1, :], in_=gnw_dram[None, h * 256:(h + 1) * 256]),
            dep_h, q="gpsimd", key=tag + "_gn")
        tgb = p.dma(lambda e, h=h: e.dma_start(
            out=brow[0:1, :], in_=gnb_dram[None, h * 256:(h + 1) * 256]),
            dep_h, q="gpsimd", key=tag + "_gn")
        ts1r = p.add("vector", lambda e: e.tensor_scalar(
            srow_b[0:1, :], grow[0:1, :], hsc[0:1, 6:7], 1.0 - LAMBDA_INIT,
            ALU.mult, ALU.mult), [tgw, hrs], small=True)
        tb1r = p.add("vector", lambda e: e.tensor_scalar(
            sqs[0:1, 0:256], grow[0:1, :], hsc[0:1, 7:8],
            -(1.0 - LAMBDA_INIT), ALU.mult, ALU.mult), [tgw, hms], small=True)
        tb2r = p.add("vector", lambda e: e.scalar_tensor_tensor(
            brow_b[0:1, :], brow[0:1, :], 1.0 - LAMBDA_INIT, sqs[0:1, 0:256],
            ALU.mult, ALU.add), [tgb, tb1r], small=True)
        tbs = p.add("tensor", lambda e: e.matmul(
            ps_bc[:, 0:256], p.ones_bf[0:1, :], srow_b[0:1, :], start=True,
            stop=True), [ts1r])
        tbsc = p.add("vector", lambda e: e.tensor_copy(sbc[:, :], ps_bc[:, 0:256]),
                     [tbs])
        tbb = p.add("tensor", lambda e: e.matmul(
            ps_bc[:, 0:256], p.ones_bf[0:1, :], brow_b[0:1, :], start=True,
            stop=True), [tbsc])
        tbbc = p.add("vector", lambda e: e.tensor_copy(bbc[:, :], ps_bc[:, 0:256]),
                     [tbb])
        avo_use = None
        prev_store = None
        for gq in range(NTILE):
            ta1 = p.add("vector", lambda e, gq=gq: e.tensor_mul(
                avo[:, :], avh[:, gq * 256:(gq + 1) * 256], sbc[:, :]),
                [tbsc, tile_done[-1]] + ([prev_store] if prev_store else []))
            ta2 = p.add("vector", lambda e: e.tensor_add(
                avo[:, :], avo[:, :], bbc[:, :]), [ta1, tbbc])
            tsv = p.dma(lambda e, h=h, gq=gq: e.dma_start(
                out=av_dram[gq * 128:(gq + 1) * 128, h * 256:(h + 1) * 256],
                in_=avo[:, :]), [ta2], key=tag + "_avst")
            prev_store = tsv
            last = tsv
        qkv_use = [p.last("tensor"), p.last("vector"), p.last("scalar"),
                   prev_store]
    fence = p.fence()
    p.release(scope)
    return last, fence


def make_ep_rope(p, cos_dram, sin_dram, style, tag):
    ct = p.sbuf(tag + "_c", [128, 64], F32)
    st = p.sbuf(tag + "_s", [128, 64], F32)
    w1 = p.sbuf(tag + "_w1", [128, 64], F32)
    w2 = p.sbuf(tag + "_w2", [128, 64], F32)
    state = dict(ti=-1, tc=None, ts=None, w_use=None)

    def ep(p_, psum, out_tile, ti, cs, cw, deps):
        if state["ti"] != ti:
            dcs = list(deps) + ([state["w_use"]] if state["w_use"] else [])
            state["tc"] = p_.dma(lambda e, ti=ti: e.dma_start(
                out=ct[:, :], in_=cos_dram[ti * 128:(ti + 1) * 128, :]), dcs,
                q="sync", key=tag + "_cs")
            state["ts"] = p_.dma(lambda e, ti=ti: e.dma_start(
                out=st[:, :], in_=sin_dram[ti * 128:(ti + 1) * 128, :]), dcs,
                q="sync", key=tag + "_cs")
            state["ti"] = ti
        d0 = deps + [state["tc"], state["ts"]]
        lastt = None
        for hb in range(cw // 128):
            b = hb * 128
            if style == "concat":
                sl1 = (slice(None), slice(b, b + 64))
                sl2 = (slice(None), slice(b + 64, b + 128))
                cc, ss = ct, st
            else:
                sl1 = (slice(None), slice(b, b + 128, 2))
                sl2 = (slice(None), slice(b + 1, b + 128, 2))
                cc, ss = ct, st
            ta = p_.add("vector", lambda e, sl1=sl1, cc=cc: e.tensor_mul(
                w1[:, :], psum[sl1], cc[:, :]), d0 + ([lastt] if lastt else []))
            tb = p_.add("vector", lambda e, sl2=sl2, ss=ss: e.tensor_mul(
                w2[:, :], psum[sl2], ss[:, :]), d0 + [ta])
            t1 = p_.add("vector", lambda e, sl1=sl1: e.tensor_sub(
                out_tile[sl1], w1[:, :], w2[:, :]), [ta, tb])
            tc2 = p_.add("vector", lambda e, sl2=sl2, cc=cc: e.tensor_mul(
                w1[:, :], psum[sl2], cc[:, :]), [t1])
            td = p_.add("vector", lambda e, sl1=sl1, ss=ss: e.tensor_mul(
                w2[:, :], psum[sl1], ss[:, :]), [tc2])
            t2 = p_.add("vector", lambda e, sl2=sl2: e.tensor_add(
                out_tile[sl2], w1[:, :], w2[:, :]), [tc2, td])
            lastt = t2
        state["w_use"] = lastt
        return lastt
    return ep


def make_ep_elu(p, tag):
    a = p.sbuf(tag + "_a", [128, 512], F32)
    t1 = p.sbuf(tag + "_t1", [128, 512], F32)
    t2 = p.sbuf(tag + "_t2", [128, 512], F32)
    state = dict(use=None)

    def ep(p_, psum, out_tile, ti, cs, cw, deps):
        d0 = list(deps) + ([state["use"]] if state["use"] else [])
        ta = p_.add("scalar", lambda e, cw=cw: e.activation(
            a[:, 0:cw], psum[:, 0:cw], AF.Copy), d0)
        tb = p_.add("vector", lambda e, cw=cw: e.tensor_scalar_min(
            t1[:, 0:cw], a[:, 0:cw], 0.0), [ta])
        tc = p_.add("scalar", lambda e, cw=cw: e.activation(
            t2[:, 0:cw], t1[:, 0:cw], AF.Exp), [tb])
        td = p_.add("vector", lambda e, cw=cw: e.tensor_scalar_max(
            t1[:, 0:cw], a[:, 0:cw], 0.0), [tc])
        te = p_.add("vector", lambda e, cw=cw: e.tensor_add(
            out_tile[:, 0:cw], t1[:, 0:cw], t2[:, 0:cw]), [td, tc])
        state["use"] = te
        return te
    return ep



def build_ln0a():
    p = Prog()
    x = p.dram("x", [T, D], F32, kind="ExternalInput")
    xn = p.dram("xn", [T, D], BF16, kind="ExternalOutput")
    emit_ln(p, x, xn, T)
    return p.build()


def build_mix0a():
    p = Prog()
    xn = p.dram("xn", [S, D], BF16, kind="ExternalInput")
    wq = p.dram("wq", [D, 1024], F32, kind="ExternalInput")
    wk = p.dram("wk", [D, 1024], F32, kind="ExternalInput")
    wv = p.dram("wv", [D, 1024], F32, kind="ExternalInput")
    bq = p.dram("bq", [1024], F32, kind="ExternalInput")
    bk = p.dram("bk", [1024], F32, kind="ExternalInput")
    bv = p.dram("bv", [1024], F32, kind="ExternalInput")
    mask = p.dram("mask", [128, 128], F32, kind="ExternalInput")
    av = p.dram("av", [S, 1024], BF16, kind="ExternalOutput")
    qd = p.dram("qs", [S, 1024], BF16)
    kd = p.dram("ks", [S, 1024], BF16)
    vd = p.dram("vs", [S, 1024], BF16)
    epq = make_ep_elu(p, "eq")
    _, f1 = emit_proj(p, xn, wq, bq, qd, S, 1024, epilogue=epq, tag="pq")
    epk = make_ep_elu(p, "ek")
    _, f2 = emit_proj(p, xn, wk, bk, kd, S, 1024, epilogue=epk, deps=f1, tag="pk")
    _, f3 = emit_proj(p, xn, wv, bv, vd, S, 1024, deps=f2, tag="pv")
    emit_linattn(p, qd, kd, vd, av, 8, S, mask, deps=f3)
    return p.build()


def build_woresffn(fold_ln):
    p = Prog()
    x = p.dram("x", [T, D], F32, kind="ExternalInput")
    av = p.dram("av", [T, D], BF16, kind="ExternalInput")
    wo = p.dram("wo", [D, D], F32, kind="ExternalInput")
    bo = p.dram("bo", [D], F32, kind="ExternalInput") if not fold_ln else None
    H = 5632
    w1 = p.dram("w1", [H, D], F32, kind="ExternalInput")
    w2 = p.dram("w2", [H, D], F32, kind="ExternalInput")
    w3 = p.dram("w3", [D, H], F32, kind="ExternalInput")
    if fold_ln:
        g = p.dram("g", [D], F32, kind="ExternalInput")
        b = p.dram("b", [D], F32, kind="ExternalInput")
    xout = p.dram("xout", [T, D], F32, kind="ExternalOutput")
    xnout = p.dram("xnout", [T, D], BF16, kind="ExternalOutput")
    x1 = p.dram("x1", [T, D], F32)
    xn1 = p.dram("xn1", [T, D], BF16)
    w1q = p.dram("w1q", [H, D], BF16)
    w2q = p.dram("w2q", [H, D], BF16)
    w3q = p.dram("w3q", [D, H], BF16)
    thr = p.dram("thr", [1, 1], F32)
    hidden = p.dram("hidden", [H, T], BF16)
    _, fq1 = emit_quantize(p, w1, H, D, w1q, thr)
    _, fq2 = emit_quantize(p, w2, H, D, w2q, thr, deps=fq1)
    _, fq3 = emit_quantize(p, w3, D, H, w3q, thr, deps=fq2)
    t1 = emit_wores(p, av, D, wo, bo, x, x1, T, deps=fq3)
    t2 = emit_ln(p, x1, xn1, T, deps=[t1])
    _, f3 = emit_bitffn(p, xn1, w1q, w2q, w3q, hidden, x1, xout, T, deps=[t2])
    if fold_ln:
        emit_ln(p, xout, xnout, T, deps=f3, fold=True, g_dram=g, b_dram=b)
    else:
        emit_ln(p, xout, xnout, T, deps=f3)
    return p.build()


def build_mix0b():
    p = Prog()
    xn = p.dram("xn", [S, D], BF16, kind="ExternalInput")
    wq = p.dram("wq", [D, 1024], F32, kind="ExternalInput")
    wk = p.dram("wk", [D, 256], F32, kind="ExternalInput")
    wv = p.dram("wv", [D, 256], F32, kind="ExternalInput")
    lam = p.dram("lam", [8], F32, kind="ExternalInput")
    wmask = p.dram("wmask", [5, 128, 640], F32, kind="ExternalInput")
    av = p.dram("av", [S, 1024], BF16, kind="ExternalOutput")
    qd = p.dram("qs", [S, 1024], BF16)
    kd = p.dram("ks", [S, 256], BF16)
    vd = p.dram("vs", [S, 256], BF16)
    cosd = p.dram("cosd", [S, 64], F32)
    sind = p.dram("sind", [S, 64], F32)
    _, f0 = emit_rope_tables(p, cosd, sind, S)
    epq = make_ep_rope(p, cosd, sind, "concat", "rq")
    _, f1 = emit_proj(p, xn, wq, None, qd, S, 1024, epilogue=epq, deps=f0, tag="pq")
    epk = make_ep_rope(p, cosd, sind, "concat", "rk")
    _, f2 = emit_proj(p, xn, wk, None, kd, S, 256, epilogue=epk, deps=f1, tag="pk")
    _, f3 = emit_proj(p, xn, wv, None, vd, S, 256, deps=f2, tag="pv")
    emit_winattn(p, qd, kd, vd, av, 8, 2, S, lam, wmask, deps=f3)
    return p.build()


def build_mix1a():
    p = Prog()
    xn = p.dram("xn", [S, D], BF16, kind="ExternalInput")
    wq = p.dram("wq", [D, 1024], F32, kind="ExternalInput")
    wk = p.dram("wk", [D, 1024], F32, kind="ExternalInput")
    wv = p.dram("wv", [D, 1024], F32, kind="ExternalInput")
    bq = p.dram("bq", [1024], F32, kind="ExternalInput")
    bk = p.dram("bk", [1024], F32, kind="ExternalInput")
    bv = p.dram("bv", [1024], F32, kind="ExternalInput")
    mask = p.dram("mask", [128, 128], F32, kind="ExternalInput")
    av = p.dram("av", [S, 1024], BF16, kind="ExternalOutput")
    qd = p.dram("qs", [S, 1024], BF16)
    kd = p.dram("ks", [S, 1024], BF16)
    vd = p.dram("vs", [S, 1024], BF16)
    _, f1 = emit_proj(p, xn, wq, bq, qd, S, 1024, tag="pq")
    _, f2 = emit_proj(p, xn, wk, bk, kd, S, 1024, deps=f1, tag="pk")
    _, f3 = emit_proj(p, xn, wv, bv, vd, S, 1024, deps=f2, tag="pv")
    emit_flash(p, qd, kd, vd, av, 8, S, 1.0 / float(np.sqrt(128.0)), mask, deps=f3)
    return p.build()


def build_woresmoe(is_last):
    p = Prog()
    E = 4096 if is_last else 2048
    F_, FS = 1408, 2816
    x = p.dram("x", [T, D], F32, kind="ExternalInput")
    av = p.dram("av", [T, E], BF16, kind="ExternalInput")
    wo = p.dram("wo", [E, D], F32, kind="ExternalInput")
    bo = None if is_last else p.dram("bo", [D], F32, kind="ExternalInput")
    gwt = p.dram("gwt", [D, 8], F32, kind="ExternalInput")
    gate_w = p.dram("gate_w", [8, F_, D], F32, kind="ExternalInput")
    up_w = p.dram("up_w", [8, F_, D], F32, kind="ExternalInput")
    down_w = p.dram("down_w", [8, D, F_], F32, kind="ExternalInput")
    shg = p.dram("shg", [FS, D], F32, kind="ExternalInput")
    shu = p.dram("shu", [FS, D], F32, kind="ExternalInput")
    shd = p.dram("shd", [D, FS], F32, kind="ExternalInput")
    xout = p.dram("xout", [T, D], F32, kind="ExternalOutput")
    if not is_last:
        xnout = p.dram("xnout", [T, D], BF16, kind="ExternalOutput")
    x1 = p.dram("x1", [T, D], F32)
    xn1 = p.dram("xn1", [T, D], BF16)
    gwt_bf = p.dram("gwt_bf", [D, 8], BF16)
    gate_bf = p.dram("gate_bf", [8, 3, 16, 128, 512], BF16)
    up_bf = p.dram("up_bf", [8, 3, 16, 128, 512], BF16)
    down_bf = p.dram("down_bf", [8, 4, 11, 128, 512], BF16)
    shg_bf = p.dram("shg_bf", [6, 16, 128, 512], BF16)
    shu_bf = p.dram("shu_bf", [6, 16, 128, 512], BF16)
    shd_bf = p.dram("shd_bf", [4, 22, 128, 512], BF16)
    hidden = p.dram("hidden", [14080, T], BF16)
    toks = []
    toks += emit_cast(p, gwt, gwt_bf, D, 8, tag="c0")
    toks += emit_cast(p, gate_w.rearrange("a b c -> (a b) c"),
                      gate_bf.rearrange("a b c -> (a b) c"), 8 * F_, D, tag="c1")
    toks += emit_cast(p, up_w.rearrange("a b c -> (a b) c"),
                      up_bf.rearrange("a b c -> (a b) c"), 8 * F_, D, tag="c2")
    toks += emit_cast(p, down_w.rearrange("a b c -> (a b) c"),
                      down_bf.rearrange("a b c -> (a b) c"), 8 * D, F_, tag="c3")
    toks += emit_cast(p, shg, shg_bf, FS, D, tag="c4")
    toks += emit_cast(p, shu, shu_bf, FS, D, tag="c5")
    toks += emit_cast(p, shd, shd_bf, D, FS, tag="c6")
    t1 = emit_wores(p, av, E, wo, bo, x, x1, T, deps=toks)
    t2 = emit_ln(p, x1, xn1, T, deps=[t1])
    _, f3 = emit_moe(p, xn1, gwt_bf, gate_bf, up_bf, down_bf, shg_bf, shu_bf,
                     shd_bf, hidden, x1, xout, T, deps=[t2])
    if not is_last:
        emit_ln(p, xout, xnout, T, deps=f3)
    return p.build()


def build_mix1b():
    p = Prog()
    xn = p.dram("xn", [S, D], BF16, kind="ExternalInput")
    wq = p.dram("wq", [D, 2048], F32, kind="ExternalInput")
    wk = p.dram("wk", [D, 512], F32, kind="ExternalInput")
    wv = p.dram("wv", [D, 512], F32, kind="ExternalInput")
    lq1 = p.dram("lq1", [128], F32, kind="ExternalInput")
    lk1 = p.dram("lk1", [128], F32, kind="ExternalInput")
    lq2 = p.dram("lq2", [128], F32, kind="ExternalInput")
    lk2 = p.dram("lk2", [128], F32, kind="ExternalInput")
    gnw = p.dram("gnw", [2048], F32, kind="ExternalInput")
    gnb = p.dram("gnb", [2048], F32, kind="ExternalInput")
    mask = p.dram("mask", [128, 128], F32, kind="ExternalInput")
    av = p.dram("av", [S, 2048], BF16, kind="ExternalOutput")
    qd = p.dram("qs", [S, 2048], BF16)
    kd = p.dram("ks", [S, 512], BF16)
    vd = p.dram("vs", [S, 512], BF16)
    cosd = p.dram("cosd", [S, 64], F32)
    sind = p.dram("sind", [S, 64], F32)
    scr = p.dram("scr", [128, 8], F32)
    _, f0 = emit_rope_tables(p, cosd, sind, S)
    epq = make_ep_rope(p, cosd, sind, "inter", "rq")
    _, f1 = emit_proj(p, xn, wq, None, qd, S, 2048, epilogue=epq, deps=f0, tag="pq")
    epk = make_ep_rope(p, cosd, sind, "inter", "rk")
    _, f2 = emit_proj(p, xn, wk, None, kd, S, 512, epilogue=epk, deps=f1, tag="pk")
    _, f3 = emit_proj(p, xn, wv, None, vd, S, 512, deps=f2, tag="pv")
    emit_diffattn(p, qd, kd, vd, av, 8, 2, S, lq1, lk1, lq2, lk2, gnw, gnb,
                  mask, scr, deps=f3)
    return p.build()



RG_PAIR = [[0, 1], [2, 3], [4, 5], [6, 7]]


def build_fused():
    p = Prog()
    EI = "ExternalInput"
    H5 = 5632
    F_, FS = 1408, 2816

    x_in = p.dram("x", [T, D], F32, kind=EI)
    mask = p.dram("mask", [128, 128], F32, kind=EI)
    wmask = p.dram("wmask", [5, 128, 640], F32, kind=EI)
    m0a_wq = p.dram("m0a_wq", [16, 2, 128, 512], F32, kind=EI)
    m0a_wk = p.dram("m0a_wk", [16, 2, 128, 512], F32, kind=EI)
    m0a_wv = p.dram("m0a_wv", [16, 2, 128, 512], F32, kind=EI)
    m0a_bq = p.dram("m0a_bq", [1024], F32, kind=EI)
    m0a_bk = p.dram("m0a_bk", [1024], F32, kind=EI)
    m0a_bv = p.dram("m0a_bv", [1024], F32, kind=EI)
    a0_wo = p.dram("a0_wo", [16, 4, 128, 512], F32, kind=EI)
    a0_bo = p.dram("a0_bo", [D], F32, kind=EI)
    f0_w1 = p.dram("f0_w1", [22528, 512], F32, kind=EI)
    f0_w2 = p.dram("f0_w2", [22528, 512], F32, kind=EI)
    f0_w3 = p.dram("f0_w3", [22528, 512], F32, kind=EI)
    m0b_wq = p.dram("m0b_wq", [16, 2, 128, 512], F32, kind=EI)
    m0b_wk = p.dram("m0b_wk", [16, 1, 128, 256], F32, kind=EI)
    m0b_wv = p.dram("m0b_wv", [16, 1, 128, 256], F32, kind=EI)
    m0b_lam = p.dram("m0b_lam", [8], F32, kind=EI)
    b0_wo = p.dram("b0_wo", [16, 4, 128, 512], F32, kind=EI)
    l1_g = p.dram("l1_g", [D], F32, kind=EI)
    l1_b = p.dram("l1_b", [D], F32, kind=EI)
    m1a_wq = p.dram("m1a_wq", [16, 2, 128, 512], F32, kind=EI)
    m1a_wk = p.dram("m1a_wk", [16, 2, 128, 512], F32, kind=EI)
    m1a_wv = p.dram("m1a_wv", [16, 2, 128, 512], F32, kind=EI)
    m1a_bq = p.dram("m1a_bq", [1024], F32, kind=EI)
    m1a_bk = p.dram("m1a_bk", [1024], F32, kind=EI)
    m1a_bv = p.dram("m1a_bv", [1024], F32, kind=EI)
    a1_wo = p.dram("a1_wo", [16, 4, 128, 512], F32, kind=EI)
    a1_bo = p.dram("a1_bo", [D], F32, kind=EI)
    moe_gwt = p.dram("moe_gwt", [D, 8], F32, kind=EI)
    moe_gate = p.dram("moe_gate", [8, 3, 16, 128, 512], F32, kind=EI)
    moe_up = p.dram("moe_up", [8, 3, 16, 128, 512], F32, kind=EI)
    moe_down = p.dram("moe_down", [8, 4, 11, 128, 512], F32, kind=EI)
    moe_shg = p.dram("moe_shg", [6, 16, 128, 512], F32, kind=EI)
    moe_shu = p.dram("moe_shu", [6, 16, 128, 512], F32, kind=EI)
    moe_shd = p.dram("moe_shd", [4, 22, 128, 512], F32, kind=EI)
    m1b_wq = p.dram("m1b_wq", [16, 4, 128, 512], F32, kind=EI)
    m1b_wk = p.dram("m1b_wk", [16, 1, 128, 512], F32, kind=EI)
    m1b_wv = p.dram("m1b_wv", [16, 1, 128, 512], F32, kind=EI)
    m1b_lq1 = p.dram("m1b_lq1", [128], F32, kind=EI)
    m1b_lk1 = p.dram("m1b_lk1", [128], F32, kind=EI)
    m1b_lq2 = p.dram("m1b_lq2", [128], F32, kind=EI)
    m1b_lk2 = p.dram("m1b_lk2", [128], F32, kind=EI)
    m1b_gnw = p.dram("m1b_gnw", [2048], F32, kind=EI)
    m1b_gnb = p.dram("m1b_gnb", [2048], F32, kind=EI)
    b1_wo = p.dram("b1_wo", [32, 4, 128, 512], F32, kind=EI)
    out = p.dram("out", [T, D], F32, kind="ExternalOutput")

    xn0 = p.dram("xn0", [T, D], BF16)
    xng0 = p.dram("xng0", [S, D], BF16)
    qd_0a = p.dram("qd_0a", [S, 1024], BF16)
    kd_0a = p.dram("kd_0a", [S, 1024], BF16)
    vd_0a = p.dram("vd_0a", [S, 1024], BF16)
    av_0a = p.dram("av_0a", [S, 1024], BF16)
    g_0a = p.dram("g_0a", [2 * S, 1024], BF16)
    at_0a = p.dram("at_0a", [S, 1024], BF16)
    w1q = p.dram("w1q", [11, 16, 128, 512], BF16)
    w2q = p.dram("w2q", [11, 16, 128, 512], BF16)
    w3q = p.dram("w3q", [4, 44, 128, 512], BF16)
    thr = p.dram("thr", [1, 1], F32)
    p3_x1 = p.dram("p3_x1", [T, D], F32)
    p3_xn1 = p.dram("p3_xn1", [T, D], BF16)
    hid3 = p.dram("hid3", [H5, T], BF16)
    x2 = p.dram("x2", [T, D], F32)
    xn2 = p.dram("xn2", [T, D], BF16)
    xng2 = p.dram("xng2", [S, D], BF16)
    cosd = p.dram("cosd", [S, 64], F32)
    sind = p.dram("sind", [S, 64], F32)
    qd_0b = p.dram("qd_0b", [S, 1024], BF16)
    kd_0b = p.dram("kd_0b", [S, 256], BF16)
    vd_0b = p.dram("vd_0b", [S, 256], BF16)
    av_0b = p.dram("av_0b", [S, 1024], BF16)
    g_0b = p.dram("g_0b", [2 * S, 1024], BF16)
    at_0b = p.dram("at_0b", [S, 1024], BF16)
    p5_x1 = p.dram("p5_x1", [T, D], F32)
    p5_xn1 = p.dram("p5_xn1", [T, D], BF16)
    hid5 = p.dram("hid5", [H5, T], BF16)
    x4 = p.dram("x4", [T, D], F32)
    xn4 = p.dram("xn4", [T, D], BF16)
    xng4 = p.dram("xng4", [S, D], BF16)
    qd_1a = p.dram("qd_1a", [S, 1024], BF16)
    kd_1a = p.dram("kd_1a", [S, 1024], BF16)
    vd_1a = p.dram("vd_1a", [S, 1024], BF16)
    av_1a = p.dram("av_1a", [S, 1024], BF16)
    g_1a = p.dram("g_1a", [2 * S, 1024], BF16)
    at_1a = p.dram("at_1a", [S, 1024], BF16)
    gwt_bf = p.dram("gwt_bf", [D, 8], BF16)
    gate_bf = p.dram("gate_bf", [8, 3, 16, 128, 512], BF16)
    up_bf = p.dram("up_bf", [8, 3, 16, 128, 512], BF16)
    down_bf = p.dram("down_bf", [8, 4, 11, 128, 512], BF16)
    shg_bf = p.dram("shg_bf", [6, 16, 128, 512], BF16)
    shu_bf = p.dram("shu_bf", [6, 16, 128, 512], BF16)
    shd_bf = p.dram("shd_bf", [4, 22, 128, 512], BF16)
    p7_x1 = p.dram("p7_x1", [T, D], F32)
    p7_xn1 = p.dram("p7_xn1", [T, D], BF16)
    hid7 = p.dram("hid7", [14080, T], BF16)
    x6 = p.dram("x6", [T, D], F32)
    xn6 = p.dram("xn6", [T, D], BF16)
    xng6 = p.dram("xng6", [S, D], BF16)
    qd_1b = p.dram("qd_1b", [S, 2048], BF16)
    kd_1b = p.dram("kd_1b", [S, 512], BF16)
    vd_1b = p.dram("vd_1b", [S, 512], BF16)
    av_1b = p.dram("av_1b", [S, 2048], BF16)
    g_1b = p.dram("g_1b", [2 * S, 2048], BF16)
    at_1b = p.dram("at_1b", [S, 2048], BF16)
    p9_x1 = p.dram("p9_x1", [T, D], F32)
    p9_xn1 = p.dram("p9_xn1", [T, D], BF16)
    hid9 = p.dram("hid9", [14080, T], BF16)
    scr = p.dram("scr", [128, 8], F32)

    def cc(kind, op, ins, outs, deps):
        return p.ccadd(lambda e, kind=kind, op=op, ins=ins, outs=outs:
                       e.collective_compute(kind, op, replica_groups=RG_PAIR,
                                            ins=[ins.opt()], outs=[outs.opt()]),
                       deps)

    def exchange(av, g, at, deps):
        t1 = cc("AllGather", ALU.bypass, av[0:T, :], g[0:S, :], deps)
        t2 = cc("AllGather", ALU.bypass, av[T:S, :], g[S:2 * S, :], deps)
        return cc("ReduceScatter", ALU.max, g[:, :], at[:, :], [t1, t2])

    def mk_src(at, ecs_per_blk):
        def src(r0, r1, ec):
            blk, ec2 = divmod(ec, ecs_per_blk)
            return at[blk * T + r0:blk * T + r1, ec2 * 128:(ec2 + 1) * 128]
        return src

    t_ln0 = emit_ln(p, x_in, xn0, T)
    tg0 = cc("AllGather", ALU.bypass, xn0[0:T, :], xng0[:, :], [t_ln0])

    _, f_rt = emit_rope_tables(p, cosd, sind, S)

    epq = make_ep_elu(p, "eq")
    _, f1 = emit_proj(p, xng0, m0a_wq, m0a_bq, qd_0a, S, 1024, epilogue=epq,
                      deps=[tg0], tag="pq")
    epk = make_ep_elu(p, "ek")
    _, f2 = emit_proj(p, xng0, m0a_wk, m0a_bk, kd_0a, S, 1024, epilogue=epk,
                      deps=f1, tag="pk")
    _, f3 = emit_proj(p, xng0, m0a_wv, m0a_bv, vd_0a, S, 1024, deps=f2, tag="pv")
    la_last, _ = emit_linattn(p, qd_0a, kd_0a, vd_0a, av_0a, 8, S, mask,
                              deps=f3, tag="at")
    tx1 = exchange(av_0a, g_0a, at_0a, [la_last])

    w1q_f = w1q.rearrange("a b c d -> (a b c) d")
    w2q_f = w2q.rearrange("a b c d -> (a b c) d")
    w3q_f = w3q.rearrange("a b c d -> (a b c) d")
    _, fq1 = emit_quantize(p, f0_w1, 22528, 512, w1q_f, thr)
    _, fq2 = emit_quantize(p, f0_w2, 22528, 512, w2q_f, thr, deps=fq1)
    _, fq3 = emit_quantize(p, f0_w3, 22528, 512, w3q_f, thr, deps=fq2)
    t_w = emit_wores(p, at_0a, 2048, a0_wo, a0_bo, x_in, p3_x1, T,
                     deps=fq3 + [tx1], av_src=mk_src(at_0a, 8))
    t_l = emit_ln(p, p3_x1, p3_xn1, T, deps=[t_w])
    _, f5 = emit_bitffn(p, p3_xn1, w1q, w2q, w3q, hid3, p3_x1, x2, T, deps=[t_l])
    t_l2 = emit_ln(p, x2, xn2, T, deps=f5)
    tg2 = cc("AllGather", ALU.bypass, xn2[0:T, :], xng2[:, :], [t_l2])

    epq2 = make_ep_rope(p, cosd, sind, "concat", "rq")
    _, f1 = emit_proj(p, xng2, m0b_wq, None, qd_0b, S, 1024, epilogue=epq2,
                      deps=f_rt + [tg2], tag="pq")
    epk2 = make_ep_rope(p, cosd, sind, "concat", "rk")
    _, f2 = emit_proj(p, xng2, m0b_wk, None, kd_0b, S, 256, epilogue=epk2,
                      deps=f1, tag="pk")
    _, f3 = emit_proj(p, xng2, m0b_wv, None, vd_0b, S, 256, deps=f2, tag="pv")
    wa_last, _ = emit_winattn(p, qd_0b, kd_0b, vd_0b, av_0b, 8, 2, S, m0b_lam,
                              wmask, deps=f3, tag="at")
    tx2 = exchange(av_0b, g_0b, at_0b, [wa_last])

    t_w = emit_wores(p, at_0b, 2048, b0_wo, None, x2, p5_x1, T, deps=[tx2],
                     av_src=mk_src(at_0b, 8))
    t_l = emit_ln(p, p5_x1, p5_xn1, T, deps=[t_w])
    _, f5 = emit_bitffn(p, p5_xn1, w1q, w2q, w3q, hid5, p5_x1, x4, T, deps=[t_l])
    t_l2 = emit_ln(p, x4, xn4, T, deps=f5, fold=True, g_dram=l1_g, b_dram=l1_b)
    tg4 = cc("AllGather", ALU.bypass, xn4[0:T, :], xng4[:, :], [t_l2])

    _, f1 = emit_proj(p, xng4, m1a_wq, m1a_bq, qd_1a, S, 1024, deps=[tg4],
                      tag="pq")
    _, f2 = emit_proj(p, xng4, m1a_wk, m1a_bk, kd_1a, S, 1024, deps=f1, tag="pk")
    _, f3 = emit_proj(p, xng4, m1a_wv, m1a_bv, vd_1a, S, 1024, deps=f2, tag="pv")
    fl_last, _ = emit_flash(p, qd_1a, kd_1a, vd_1a, av_1a, 8, S,
                            1.0 / float(np.sqrt(128.0)), mask, deps=f3, tag="at")
    tx3 = exchange(av_1a, g_1a, at_1a, [fl_last])

    ctoks = []
    r5 = "a b c d e -> (a b c d) e"
    r4 = "a b c d -> (a b c) d"
    ctoks += emit_cast(p, moe_gwt, gwt_bf, D, 8, tag="cast")
    ctoks += emit_cast(p, moe_gate.rearrange(r5), gate_bf.rearrange(r5),
                       8 * 3 * 16 * 128, 512, tag="cast")
    ctoks += emit_cast(p, moe_up.rearrange(r5), up_bf.rearrange(r5),
                       8 * 3 * 16 * 128, 512, tag="cast")
    ctoks += emit_cast(p, moe_down.rearrange(r5), down_bf.rearrange(r5),
                       8 * 4 * 11 * 128, 512, tag="cast")
    ctoks += emit_cast(p, moe_shg.rearrange(r4), shg_bf.rearrange(r4),
                       6 * 16 * 128, 512, tag="cast")
    ctoks += emit_cast(p, moe_shu.rearrange(r4), shu_bf.rearrange(r4),
                       6 * 16 * 128, 512, tag="cast")
    ctoks += emit_cast(p, moe_shd.rearrange(r4), shd_bf.rearrange(r4),
                       4 * 22 * 128, 512, tag="cast")
    t_w = emit_wores(p, at_1a, 2048, a1_wo, a1_bo, x4, p7_x1, T,
                     deps=ctoks + [tx3], av_src=mk_src(at_1a, 8))
    t_l = emit_ln(p, p7_x1, p7_xn1, T, deps=[t_w])
    _, f5 = emit_moe(p, p7_xn1, gwt_bf, gate_bf, up_bf, down_bf, shg_bf, shu_bf,
                     shd_bf, hid7, p7_x1, x6, T, deps=[t_l])
    t_l2 = emit_ln(p, x6, xn6, T, deps=f5)
    tg6 = cc("AllGather", ALU.bypass, xn6[0:T, :], xng6[:, :], [t_l2])

    epq3 = make_ep_rope(p, cosd, sind, "inter", "rq")
    _, f1 = emit_proj(p, xng6, m1b_wq, None, qd_1b, S, 2048, epilogue=epq3,
                      deps=f_rt + [tg6], tag="pq")
    epk3 = make_ep_rope(p, cosd, sind, "inter", "rk")
    _, f2 = emit_proj(p, xng6, m1b_wk, None, kd_1b, S, 512, epilogue=epk3,
                      deps=f1, tag="pk")
    _, f3 = emit_proj(p, xng6, m1b_wv, None, vd_1b, S, 512, deps=f2, tag="pv")
    da_last, _ = emit_diffattn(p, qd_1b, kd_1b, vd_1b, av_1b, 8, 2, S,
                               m1b_lq1, m1b_lk1, m1b_lq2, m1b_lk2,
                               m1b_gnw, m1b_gnb, mask, scr, deps=f3, tag="at")
    tx4 = exchange(av_1b, g_1b, at_1b, [da_last])

    t_w = emit_wores(p, at_1b, 4096, b1_wo, None, x6, p9_x1, T, deps=[tx4],
                     av_src=mk_src(at_1b, 16))
    t_l = emit_ln(p, p9_x1, p9_xn1, T, deps=[t_w])
    emit_moe(p, p9_xn1, gwt_bf, gate_bf, up_bf, down_bf, shg_bf, shu_bf,
             shd_bf, hid9, p9_x1, out, T, deps=[t_l], w3key="mo2_w3")

    import os
    return p.build(debug_serialize=os.environ.get("SER", "0") == "1")



MASK_KQ = np.ascontiguousarray(
    (np.arange(128)[:, None] <= np.arange(128)[None, :]).astype(np.float32))


def _win_masks():
    m = np.zeros((5, 128, 640), np.float32)
    for mi in range(5):
        i = np.arange(128)[:, None]
        j = np.arange(640)[None, :]
        kg = (mi - 4) * 128 + j if mi >= 4 else j
        d = mi * 128 + i - kg
        m[mi] = np.where((d >= 0) & (d <= 511), 0.0, -1.0e9)
    return m


WIN_MASKS = _win_masks()


def _cc(a):
    return np.ascontiguousarray(a)


def _pack_T(w, fsz=512):
    R, C = w.shape
    nf = -(-R // fsz)
    if nf * fsz != R:
        wpad = np.zeros((nf * fsz, C), w.dtype)
        wpad[:R] = w
        w = wpad
    nd = C // 128
    return np.ascontiguousarray(w.T.reshape(nd, 128, nf, fsz).transpose(2, 0, 1, 3))


def _pack_w(w):
    R, C = w.shape
    csz = min(512, C)
    return np.ascontiguousarray(
        w.reshape(R // 128, 128, C // csz, csz).transpose(0, 2, 1, 3))


def kernel(**inp):
    nc = build_fused()
    x0 = inp["L0_attn_a_x"]
    moe_gwt = _cc(inp["L1_ffn_gate_weight"].T)
    f0_w1 = _pack_T(inp["L0_ffn_w1"]).reshape(-1, 512)
    f0_w2 = _pack_T(inp["L0_ffn_w2"]).reshape(-1, 512)
    f0_w3 = _pack_T(inp["L0_ffn_w3"]).reshape(-1, 512)
    moe_gate = np.stack([_pack_T(inp["L1_ffn_gate_w"][e]) for e in range(8)])
    moe_up = np.stack([_pack_T(inp["L1_ffn_up_w"][e]) for e in range(8)])
    moe_down = np.stack([_pack_T(inp["L1_ffn_down_w"][e]) for e in range(8)])
    moe_shg = _pack_T(inp["L1_ffn_sh_gate_w"])
    moe_shu = _pack_T(inp["L1_ffn_sh_up_w"])
    moe_shd = _pack_T(inp["L1_ffn_sh_down_w"])
    a0_wo = _pack_w(inp["L0_attn_a_Wo"])
    b0_wo = _pack_w(inp["L0_attn_b_Wo"])
    a1_wo = _pack_w(inp["L1_attn_a_Wo"])
    b1_wo = _pack_w(inp["L1_attn_b_Wo"])

    maps = []
    for c in range(NCORE):
        b = c >> 1
        hg = c & 1
        hc = slice(hg * 1024, (hg + 1) * 1024)
        maps.append({
            "x": _cc(x0[b, hg * T:(hg + 1) * T]),
            "mask": MASK_KQ,
            "wmask": WIN_MASKS,
            "m0a_wq": _pack_w(inp["L0_attn_a_Wq"][:, hc]),
            "m0a_wk": _pack_w(inp["L0_attn_a_Wk"][:, hc]),
            "m0a_wv": _pack_w(inp["L0_attn_a_Wv"][:, hc]),
            "m0a_bq": _cc(inp["L0_attn_a_bq"][hc]),
            "m0a_bk": _cc(inp["L0_attn_a_bk"][hc]),
            "m0a_bv": _cc(inp["L0_attn_a_bv"][hc]),
            "a0_wo": a0_wo,
            "a0_bo": _cc(inp["L0_attn_a_bo"]),
            "f0_w1": f0_w1,
            "f0_w2": f0_w2,
            "f0_w3": f0_w3,
            "m0b_wq": _pack_w(inp["L0_attn_b_Wq"][:, hc]),
            "m0b_wk": _pack_w(inp["L0_attn_b_Wk"][:, hg * 256:(hg + 1) * 256]),
            "m0b_wv": _pack_w(inp["L0_attn_b_Wv"][:, hg * 256:(hg + 1) * 256]),
            "m0b_lam": _cc(inp["L0_attn_b_lam"][hg * 8:(hg + 1) * 8]),
            "b0_wo": b0_wo,
            "l1_g": _cc(inp["L1_attn_a_ln_g"]),
            "l1_b": _cc(inp["L1_attn_a_ln_b"]),
            "m1a_wq": _pack_w(inp["L1_attn_a_Wq"][:, hc]),
            "m1a_wk": _pack_w(inp["L1_attn_a_Wk"][:, hc]),
            "m1a_wv": _pack_w(inp["L1_attn_a_Wv"][:, hc]),
            "m1a_bq": _cc(inp["L1_attn_a_bq"][hc]),
            "m1a_bk": _cc(inp["L1_attn_a_bk"][hc]),
            "m1a_bv": _cc(inp["L1_attn_a_bv"][hc]),
            "a1_wo": a1_wo,
            "a1_bo": _cc(inp["L1_attn_a_bo"]),
            "moe_gwt": moe_gwt,
            "moe_gate": moe_gate,
            "moe_up": moe_up,
            "moe_down": moe_down,
            "moe_shg": moe_shg,
            "moe_shu": moe_shu,
            "moe_shd": moe_shd,
            "m1b_wq": _pack_w(inp["L1_attn_b_Wq"][:, hg * 2048:(hg + 1) * 2048]),
            "m1b_wk": _pack_w(inp["L1_attn_b_Wk"][:, hg * 512:(hg + 1) * 512]),
            "m1b_wv": _pack_w(inp["L1_attn_b_Wv"][:, hg * 512:(hg + 1) * 512]),
            "m1b_lq1": _cc(inp["L1_attn_b_lambda_q1"]),
            "m1b_lk1": _cc(inp["L1_attn_b_lambda_k1"]),
            "m1b_lq2": _cc(inp["L1_attn_b_lambda_q2"]),
            "m1b_lk2": _cc(inp["L1_attn_b_lambda_k2"]),
            "m1b_gnw": _cc(inp["L1_attn_b_gn_w"][hg * 2048:(hg + 1) * 2048]),
            "m1b_gnb": _cc(inp["L1_attn_b_gn_b"][hg * 2048:(hg + 1) * 2048]),
            "b1_wo": b1_wo,
        })

    res = _run(nc, maps, has_inv=True)

    out = np.zeros((B, S, D), np.float32)
    for c in range(NCORE):
        out[c >> 1, (c & 1) * T:((c & 1) + 1) * T] = res.results[c]["out"]
    return out
